# version 18
# baseline (speedup 1.0000x reference)
import numpy as np
import ml_dtypes
from contextlib import ExitStack

import concourse.bass as bass
import concourse.mybir as mybir
from concourse.bass_utils import run_bass_kernel_spmd

F32 = mybir.dt.float32
BF16 = mybir.dt.bfloat16
I32 = mybir.dt.int32
AF = mybir.ActivationFunctionType
ALU = mybir.AluOpType
BF = ml_dtypes.bfloat16

D = 1024
N = 4096
NOWN = 2048
H = 4
FF = 4096
HID = 3072
NT = 512
CH = 8
EPS = 1e-5
S127 = float(np.float32(1.0 / 127.0))
ENGINES = ("sync", "tensor", "vector", "scalar", "gpsimd")


def ap3(h, off, dims):
    return bass.AP(h, off, [list(d) for d in dims])


class B:
    def __init__(self):
        self.nc = bass.Bass("TRN2", target_bir_lowering=False)
        self.es = ExitStack()
        self.sem = {}
        self.cnt = {}
        for s in ("te", "ve", "se", "gp2", "syA", "syB", "gpA", "gpB"):
            self.sem[s] = self.es.enter_context(self.nc.semaphore(s))
            self.cnt[s] = 0
        self.q = None
        self.hwm = None

    def begin(self, name="ph"):
        assert self.q is None
        self.q = {e: [] for e in ENGINES}
        self.hwm = {e: {} for e in ENGINES}
        self.phname = name

    def end(self):
        qs, self.q = self.q, None
        with self.nc.named_scope(self.phname):
            with self.nc.Block() as blk:
                for e in ENGINES:
                    if qs[e]:
                        def runner(eng, _ops=tuple(qs[e])):
                            for f in _ops:
                                f(eng)
                        getattr(blk, e)(runner)

    OWN = {"vector": "ve", "scalar": "se", "gpsimd": "gp2"}

    def op(self, eng, fn, waits=(), inc=None, inck=1):
        own = self.OWN.get(eng)
        if own is not None:
            waits = list(waits) + [(own, self.cnt[own])]
            if inc is None:
                inc, inck = own, 1
        w = []
        hw = self.hwm[eng]
        for s, v in waits:
            if not v:
                continue
            if hw.get(s, 0) < v:
                hw[s] = v
                w.append((self.sem[s], v))
        semh = self.sem[inc] if inc else None
        if inc:
            self.cnt[inc] += inck

        def run(engine, _w=tuple(w), _fn=fn, _s=semh, _k=inck):
            for sh, v in _w:
                engine.wait_ge(sh, v)
            inst = _fn(engine)
            if _s is not None:
                inst.then_inc(_s, _k)
        self.q[eng].append(run)
        return self.cnt[inc] if inc else None


class Pool:
    def __init__(self, items):
        self.items = [{"ap": a, "free": None} for a in items]
        self.i = 0

    def acquire(self):
        r = self.items[self.i % len(self.items)]
        self.i += 1
        return r

    @staticmethod
    def rel(r, sem, val):
        r["free"] = (sem, val)

    @staticmethod
    def w(r):
        return [r["free"]] if r["free"] else []


class DmaQ:

    def __init__(self, b, eng, semA, semB):
        self.b, self.eng, self.sems = b, eng, (semA, semB)
        self.k = 0

    def dma(self, fn, waits=()):
        k = self.k
        self.k += 1
        sem = self.sems[k % 2]
        w = list(waits) + ([(sem, 16 * (k // 2))] if k >= 2 else [])
        self.b.op(self.eng, fn, waits=w, inc=sem, inck=16)
        return (sem, 16 * (k // 2 + 1))


def build_nc(debug=False, phases=99):
    b = B()
    nc = b.nc
    dbg = {}

    xkv = nc.dram_tensor("xkv", [D, N], F32, kind="ExternalInput")
    wq_d = nc.dram_tensor("wq", [D, D], F32, kind="ExternalInput")
    wk_d = nc.dram_tensor("wk", [D, D], F32, kind="ExternalInput")
    wv_d = nc.dram_tensor("wv", [D, D], F32, kind="ExternalInput")
    wo_d = nc.dram_tensor("wo", [D, D], F32, kind="ExternalInput")
    wff1_d = nc.dram_tensor("wff1", [D, FF], F32, kind="ExternalInput")
    wff2_d = nc.dram_tensor("wff2", [FF, D], F32, kind="ExternalInput")
    wdd_d = nc.dram_tensor("wdd", [D, D], F32, kind="ExternalInput")
    wgT_d = nc.dram_tensor("wgT", [D, HID], BF16, kind="ExternalInput")
    wuT_d = nc.dram_tensor("wuT", [D, HID], BF16, kind="ExternalInput")
    wdT_d = nc.dram_tensor("wdT", [HID, D], BF16, kind="ExternalInput")
    pvec_d = nc.dram_tensor("pvec", [128, 90], F32, kind="ExternalInput")
    cbf_d = nc.dram_tensor("cbf", [128, 546], BF16, kind="ExternalInput")
    yout = nc.dram_tensor("y", [D, NOWN], F32, kind="ExternalOutput")

    es = b.es

    def sbt(name, shape, dtype):
        return es.enter_context(nc.sbuf_tensor(name, shape, dtype))

    def alias(name, shape, dtype, base, off_bytes=0):
        return nc.alloc_sbuf_tensor_at(name, shape, dtype,
                                       offset=nc.lookup_mloc(base).addr + off_bytes)

    h1 = sbt("h1", [128, CH * N], BF16)
    R1 = alias("R1", [128, CH * NOWN], F32, h1)
    Q = sbt("Q", [128, CH * NOWN], BF16)
    h2x = alias("h2x", [128, CH * NOWN], BF16, Q)
    xq0 = alias("xq0", [128, CH * NT], F32, Q)
    xq1 = alias("xq1", [128, CH * NT], F32, Q, 16384)
    Dr = sbt("Dr", [128, 16384], BF16)
    CWt = alias("CW", [128, CH * D], BF16, Dr)
    slotD0 = alias("slotD0", [128, 8192], BF16, Dr)
    slotD1 = alias("slotD1", [128, 8192], BF16, Dr, 16384)
    shad = alias("shad", [128, CH * NT], BF16, Dr)
    shad2 = alias("shad2", [128, CH * NT], BF16, Dr, 8192)
    Ftn = Dr
    estR = sbt("estR", [128, 8192], F32)
    Gq = alias("Gq", [128, 8 * NOWN], BF16, estR)
    wslotE0 = alias("wslotE0", [128, 8192], BF16, estR)
    wslotE1 = alias("wslotE1", [128, 8192], BF16, estR, 16384)
    ebf0 = alias("ebf0", [128, 8192], BF16, estR)
    ebf1 = alias("ebf1", [128, 8192], BF16, estR, 16384)
    WARE = sbt("WARE", [128, 16384], BF16)
    W0 = alias("W0", [128, 8192], BF16, WARE)
    W1 = alias("W1", [128, 8192], BF16, WARE, 16384)
    sbb = sbt("sbb", [128, 2048], BF16)
    Csb = alias("Csb", [128, 2048], BF16, sbb)
    slbt = sbt("slbt", [128, 1024], BF16)
    tmp0 = sbt("tmp0", [128, NT], F32)
    tmp1 = sbt("tmp1", [128, NT], F32)
    rsgb = alias("rsgb", [128, NT], BF16, tmp1)
    rowb = sbt("rowb", [1, 2048], BF16)
    qrecf = alias("qrecf", [4, 512], F32, tmp0)
    qrecb = alias("qrecb", [4, 1024], BF16, rowb)
    invSpp = sbt("invSpp", [128, 8], F32)
    pv = sbt("pv", [128, 90], F32)
    cb = sbt("cb", [128, 546], BF16)
    pbs = [es.enter_context(nc.psum_tensor(f"pb{i}", [128, 512], F32)) for i in range(8)]
    bst = [{"ap": p, "free": None} for p in pbs]

    def mkpool(ixs):
        pl = Pool.__new__(Pool)
        pl.items = [bst[i] for i in ixs]
        pl.i = 0
        return pl

    onescol = cb[:, 0:1]
    qind = lambda c: cb[:, 2 + 4 * c: 2 + 4 * (c + 1)]
    sel4 = lambda h: cb[0:4, 34 + 128 * h: 34 + 128 * (h + 1)]
    orow = cb[0:1, 34:162]
    ln1g = lambda c: pv[:, 0 + c: 1 + c]
    ln1b = lambda c: pv[:, 8 + c: 9 + c]
    ln2g = lambda c: pv[:, 16 + c: 17 + c]
    ln2b = lambda c: pv[:, 24 + c: 25 + c]
    bo_c = lambda c: pv[:, 32 + c: 33 + c]
    bff2_c = lambda c: pv[:, 40 + c: 41 + c]
    bd_c = lambda c: pv[:, 48 + c: 49 + c]
    bff1_c = lambda j: pv[:, 56 + j: 57 + j]
    epsap = pv[0:1, 88:89]
    onef = pv[0:1, 89:90]

    syQ = DmaQ(b, "sync", "syA", "syB")
    gpQ = DmaQ(b, "gpsimd", "gpA", "gpB")

    def load_x(dst, off, col0, width=NT, waits=()):
        return syQ.dma(lambda e: e.dma_start(
            ap3(dst, off, [[dst.shape[1], 128], [width, CH], [1, width]]),
            ap3(xkv, col0, [[N, 128], [128 * N, CH], [1, width]])), waits=waits)

    def load_w(dst, dram, cols, kc_cnt, row0=0, col0=0, width=1024, waits=(),
               q=None):
        return (q or gpQ).dma(lambda e: e.dma_start(
            ap3(dst, 0, [[dst.shape[1], 128], [width, kc_cnt], [1, width]]),
            ap3(dram, row0 * cols + col0, [[cols, 128], [128 * cols, kc_cnt], [1, width]])),
            waits=waits)

    ln_state = {"sbb_free0": 0, "sbb_free1": 0,
                "tmp_free0": ("se", 0), "tmp_free1": ("se", 0),
                "rowb_free0": 0, "rowb_free1": 0, "rowb_free2": 0, "rowb_free3": 0,
                "musq_free": 0}
    consts_edges = []

    def ln_batch(tiles, stat_pool, bcm_pool, bcr_pool):
        infos = []
        for ti in tiles:
            xw = list(ti.get("x_waits", ()))
            sbk = stat_pool.acquire()
            bank = sbk["ap"]
            xsrc_c = ti["xsrc_c"]
            for c in range(CH):
                xb = sbb[:, (c % 2) * 512:(c % 2) * 512 + 512]
                xq = sbb[:, 1024 + (c % 2) * 512:1024 + (c % 2) * 512 + 512]
                vw = xw + [("te", ln_state[f"sbb_free{c % 2}"])]
                ve1 = b.op("vector", lambda e, c=c, xb=xb, f=xsrc_c: e.tensor_copy(
                    xb, f(c)), waits=vw, inc="ve")
                ve2 = b.op("vector", lambda e, xb=xb, xq=xq: e.tensor_mul(xq, xb, xb),
                           inc="ve")
                b.op("tensor", lambda e, xb=xb, bank=bank, c=c: e.matmul(
                    bank[0:1, :], onescol, xb, start=(c == 0), stop=False,
                    skip_group_check=True),
                    waits=[("ve", ve1)] + (Pool.w(sbk) + consts_edges if c == 0 else []))
                te = b.op("tensor", lambda e, xq=xq, bank=bank, c=c: e.matmul(
                    bank[32:33, :], onescol, xq, start=(c == 0), stop=False,
                    skip_group_check=True), waits=[("ve", ve2)], inc="te")
                ln_state[f"sbb_free{c % 2}"] = te
            infos.append({"sbk": sbk, "stat_done": b.cnt["te"]})
        def rows(i):
            inf = infos[i]
            bank = inf["sbk"]["ap"]
            slot = i % 2
            mu_ap = rowb[0:1, slot * 1024: slot * 1024 + 512]
            rsig_ap = rowb[0:1, slot * 1024 + 512: slot * 1024 + 1024]
            se1 = b.op("scalar", lambda e, bank=bank, mu_ap=mu_ap: e.activation(
                mu_ap, bank[0:1, :], AF.Identity, scale=1.0 / 1024),
                waits=[("te", inf["stat_done"]), ("te", ln_state[f"rowb_free{slot}"])])
            b.op("vector", lambda e, mu_ap=mu_ap: e.tensor_mul(
                tmp0[0:1, :], mu_ap, mu_ap),
                waits=[("se", se1), ("se", ln_state["musq_free"]),
                       ln_state["tmp_free0"]])
            ve3 = b.op("vector", lambda e, bank=bank: e.scalar_tensor_tensor(
                bank[32:33, :], bank[32:33, :], 1.0 / 1024, tmp0[0:1, :],
                ALU.mult, ALU.subtract), inc="ve")
            b.op("scalar", lambda e, bank=bank: e.activation(
                bank[0:1, :], bank[32:33, :], AF.Ln, bias=epsap),
                waits=[("ve", ve3)])
            se2 = b.op("scalar", lambda e, bank=bank, rsig_ap=rsig_ap: e.activation(
                rsig_ap, bank[0:1, :], AF.Exp, scale=-0.5), inc="se")
            ln_state["musq_free"] = se2
            Pool.rel(inf["sbk"], "se", se2)
            inf.update(slot=slot, mu_ap=mu_ap, rsig_ap=rsig_ap, rows_done=se2)
        rows(0)
        for i, ti in enumerate(tiles):
            if i + 1 < len(tiles):
                rows(i + 1)
            inf = infos[i]
            bm = bcm_pool.acquire()
            b.op("tensor", lambda e, bank=bm["ap"], mu_ap=inf["mu_ap"]: e.matmul(
                bank[:, :], orow, mu_ap, start=True, stop=True,
                skip_group_check=True),
                waits=[("se", inf["rows_done"])] + Pool.w(bm))
            br = bcr_pool.acquire()
            b.op("tensor", lambda e, bank=br["ap"], rsig_ap=inf["rsig_ap"]: e.matmul(
                bank[:, :], orow, rsig_ap, start=True, stop=True,
                skip_group_check=True), waits=Pool.w(br), inc="te")
            bc_done = b.cnt["te"]
            ln_state[f"rowb_free{inf['slot']}"] = bc_done
            xsrc_c = ti["xsrc_c"]
            apply_out = ti["apply_out"]
            for c in range(CH):
                tmp = tmp0 if c % 2 == 0 else tmp1
                b.op("vector", lambda e, c=c, tmp=tmp, bank=bm["ap"], f=xsrc_c:
                     e.tensor_sub(tmp[:, :], f(c), bank[:, :]),
                     waits=[("te", bc_done), ln_state[f"tmp_free{c % 2}"]],
                     inc="ve")
                apply_out(c, tmp, br["ap"])
            Pool.rel(bm, "ve", b.cnt["ve"])
            Pool.rel(br, "ve", b.cnt["ve"])

    b.begin("PA")
    consts_edges.append(syQ.dma(lambda e: e.dma_start(pv[:, :], pvec_d.ap())))
    consts_edges.append(syQ.dma(lambda e: e.dma_start(cb[:, :], cbf_d.ap())))
    gk = load_w(W1, wk_d, D, CH)
    gv = load_w(slotD1, wv_d, D, CH)
    gq = load_w(W0, wq_d, D, CH)

    lnb = pbs[7]
    st = {"bank_free": 0, "h1": [0] * 8, "xedge": [None] * 8,
          "xfree": [0, 0], "stat_done": [0] * 8, "rows": [None] * 8}
    st["xedge"][0] = load_x(xq0, 0, 0, waits=())
    st["xedge"][1] = load_x(xq1, 0, NT, waits=())
    xsl = [xq0, xq1]

    def lnA(t):
        xs = xsl[t % 2]
        for c in range(CH):
            xb = sbb[:, (c % 2) * 512:(c % 2) * 512 + 512]
            xq_ = sbb[:, 1024 + (c % 2) * 512:1024 + (c % 2) * 512 + 512]
            vw = [st["xedge"][t], ("te", ln_state[f"sbb_free{c % 2}"])]
            ve1 = b.op("vector", lambda e, c=c, xb=xb, xs=xs: e.tensor_copy(
                xb, xs[:, c * NT:(c + 1) * NT]), waits=vw, inc="ve")
            ve2 = b.op("scalar", lambda e, xb=xb, xq_=xq_: e.activation(
                xq_, xb, AF.Square), waits=[("ve", ve1)], inc="se")
            b.op("tensor", lambda e, xb=xb, c=c: e.matmul(
                lnb[0:1, :], onescol, xb, start=(c == 0), stop=False,
                skip_group_check=True),
                waits=[("ve", ve1)] + ([("ve", st["bank_free"])] + consts_edges
                                       if c == 0 else []))
            te = b.op("tensor", lambda e, xq_=xq_, c=c: e.matmul(
                lnb[32:33, :], onescol, xq_, start=(c == 0), stop=False,
                skip_group_check=True), waits=[("se", ve2)], inc="te")
            ln_state[f"sbb_free{c % 2}"] = te
        st["stat_done"][t] = b.cnt["te"]

    def lnB1(t):
        slot = t % 2
        mu_ap = rowb[0:1, slot * 1024: slot * 1024 + 512]
        rsig_ap = rowb[0:1, slot * 1024 + 512: slot * 1024 + 1024]
        st["mu_ap"], st["rsig_ap"] = mu_ap, rsig_ap
        se1 = b.op("scalar", lambda e, mu_ap=mu_ap: e.activation(
            mu_ap, lnb[0:1, :], AF.Identity, scale=1.0 / 1024),
            waits=[("te", st["stat_done"][t]),
                   ("te", ln_state[f"rowb_free{slot}"])])
        b.op("vector", lambda e, mu_ap=mu_ap: e.tensor_mul(
            tmp0[0:1, :], mu_ap, mu_ap),
            waits=[("se", se1), ("se", ln_state["musq_free"])])
        ve3 = b.op("vector", lambda e: e.scalar_tensor_tensor(
            lnb[32:33, :], lnb[32:33, :], 1.0 / 1024, tmp0[0:1, :],
            ALU.mult, ALU.subtract), inc="ve")
        st["tmprow"] = ve3
        st["ve3"] = ve3
        b.op("scalar", lambda e: e.activation(
            lnb[0:1, :], lnb[32:33, :], AF.Ln, bias=epsap), waits=[("ve", ve3)])
        se2 = b.op("scalar", lambda e, rsig_ap=rsig_ap: e.activation(
            rsig_ap, lnb[0:1, :], AF.Exp, scale=-0.5), inc="se")
        ln_state["musq_free"] = se2
        st["se2"] = se2

    def lnB2(t):
        xs = xsl[t % 2]
        slot = t % 2
        b.op("tensor", lambda e, mu_ap=st["mu_ap"]: e.matmul(
            lnb[:, :], orow, mu_ap, start=True, stop=True,
            skip_group_check=True),
            waits=[("se", st["se2"]), ("ve", st["ve3"])], inc="te")
        mubc = b.cnt["te"]
        ln_state[f"rowb_free{slot}"] = mubc
        for c in range(CH):
            b.op("vector", lambda e, c=c, xs=xs: e.tensor_sub(
                xs[:, c * NT:(c + 1) * NT], xs[:, c * NT:(c + 1) * NT],
                lnb[:, :]), waits=[("te", mubc)], inc="ve")
        st["p1ve"] = b.cnt["ve"]

    def lnB3(t):
        xs = xsl[t % 2]
        slot = t % 2
        b.op("tensor", lambda e, rsig_ap=st["rsig_ap"]: e.matmul(
            lnb[:, :], orow, rsig_ap, start=True, stop=True,
            skip_group_check=True), waits=[("ve", st["p1ve"])], inc="te")
        rsbc = b.cnt["te"]
        ln_state[f"rowb_free{slot}"] = rsbc
        for c in range(CH):
            b.op("vector", lambda e, c=c, t=t, xs=xs: e.scalar_tensor_tensor(
                h1[:, c * N + t * NT: c * N + (t + 1) * NT],
                xs[:, c * NT:(c + 1) * NT], ln1g(c), lnb[:, :],
                ALU.mult, ALU.mult), waits=[("te", rsbc)], inc="ve")
            b.op("vector", lambda e, c=c, t=t: e.tensor_scalar_add(
                h1[:, c * N + t * NT: c * N + (t + 1) * NT],
                h1[:, c * N + t * NT: c * N + (t + 1) * NT],
                ln1b(c)), inc="ve")
        st["bank_free"] = b.cnt["ve"]
        st["xfree"][slot] = b.cnt["ve"]
        st["h1"][t] = b.cnt["ve"]
        if t + 2 < 8:
            st["xedge"][t + 2] = load_x(xsl[t % 2], 0, (t + 2) * NT,
                                        waits=[("ve", st["xfree"][slot])])

    rot = mkpool([5, 6])
    ev = Pool([ebf0, ebf1])

    def kv_ssub(t, s_, eslab, first_tile_s):
        n0 = t * NT + s_ * 128
        res = {}
        for half in range(2):
            bk = rot.acquire()
            for kc in range(CH):
                w = (Pool.w(bk) + [gk, ("ve", st["h1"][t])]) if kc == 0 else []
                b.op("tensor", lambda e, kc=kc, half=half, n0=n0, bk=bk: e.matmul(
                    bk["ap"][:, :],
                    h1[:, kc * N + n0: kc * N + n0 + 128],
                    W1[:, kc * 1024 + half * 512: kc * 1024 + (half + 1) * 512],
                    start=(kc == 0), stop=(kc == 7)), waits=w,
                    inc="te" if kc == 7 else None)
            tek = b.cnt["te"]
            se = b.op("scalar", lambda e, s_=s_, half=half, eslab=eslab, bk=bk:
                      e.activation(
                eslab[:, s_ * 1024 + half * 512: s_ * 1024 + (half + 1) * 512],
                bk["ap"][:, :], AF.Exp, scale=0.25), waits=[("te", tek)], inc="se")
            Pool.rel(bk, "se", se)
            res[f"e{half}"] = se
            bv = rot.acquire()
            for kc in range(CH):
                w = (Pool.w(bv) + [gv]) if kc == 0 else []
                b.op("tensor", lambda e, kc=kc, half=half, n0=n0, bv=bv: e.matmul(
                    bv["ap"][:, :],
                    h1[:, kc * N + n0: kc * N + n0 + 128],
                    slotD1[:, kc * 1024 + half * 512: kc * 1024 + (half + 1) * 512],
                    start=(kc == 0), stop=(kc == 7)), waits=w,
                    inc="te" if kc == 7 else None)
            tev = b.cnt["te"]
            ve = b.op("scalar", lambda e, s_=s_, half=half, eslab=eslab, bv=bv:
                      e.activation(
                eslab[:, 4096 + s_ * 1024 + half * 512:
                      4096 + s_ * 1024 + (half + 1) * 512],
                bv["ap"][:, :], AF.Identity), waits=[("te", tev)], inc="se")
            Pool.rel(bv, "se", ve)
            res[f"v{half}"] = ve
        first = first_tile_s
        for c in range(CH):
            b.op("tensor", lambda e, s_=s_, c=c, eslab=eslab, first=first: e.matmul(
                pbs[4][:, c:c + 1],
                eslab[:, s_ * 1024 + c * 128: s_ * 1024 + (c + 1) * 128],
                onescol,
                start=first, stop=False, skip_group_check=True),
                waits=[("se", res[f"e{c // 4}"])] + (consts_edges if first else []))
        for h in range(H):
            for ec in range(2):
                b.op("tensor", lambda e, s_=s_, h=h, ec=ec, eslab=eslab,
                     first=first: e.matmul(
                    pbs[h][:, ec * 256: (ec + 1) * 256],
                    eslab[:, 4096 + s_ * 1024 + h * 256 + ec * 128:
                          4096 + s_ * 1024 + h * 256 + (ec + 1) * 128],
                    eslab[:, s_ * 1024 + h * 256: s_ * 1024 + (h + 1) * 256],
                    start=first, stop=False, skip_group_check=True),
                    waits=[("se", res["v1"])],
                    inc="te" if (h == H - 1 and ec == 1) else None)

    lnA(0)
    lnB1(0)
    lnB2(0)
    lnB3(0)
    lnA(1)
    lnB1(1)
    lnB2(1)
    lnB3(1)
    for t in range(8):
        slot = ev.acquire()
        eslab = slot["ap"]
        kv_ssub(t, 0, eslab, t == 0)
        if t + 2 < 8:
            lnA(t + 2)
        kv_ssub(t, 1, eslab, False)
        if t + 2 < 8:
            lnB1(t + 2)
        kv_ssub(t, 2, eslab, False)
        if t + 2 < 8:
            lnB2(t + 2)
        kv_ssub(t, 3, eslab, False)
        if t + 2 < 8:
            lnB3(t + 2)
        Pool.rel(slot, "te", b.cnt["te"])
    kv_te = b.cnt["te"]
    p0_rows_done = max(ln_state["rowb_free0"], ln_state["rowb_free1"])
    p0_tmp_done = st["tmprow"]
    p0_sbb_done = max(ln_state["sbb_free0"], ln_state["sbb_free1"])

    qb_free = [0, 0]
    pb7f = [("ve", st["bank_free"])]
    pend = [None]

    def q_chain_a(P):
        par = P["par"]
        b.op("scalar", lambda e: e.activation(
            qrecf[0:4, :], pbs[7][0:4, :], AF.Ln),
            waits=[("te", P["teqs"]), ("ve", p0_tmp_done)])
        se = b.op("scalar", lambda e, par=par: e.activation(
            qrecb[0:4, par * 512:(par + 1) * 512], qrecf[0:4, :], AF.Exp,
            scale=-1.0), waits=[("te", p0_rows_done), ("te", qb_free[par])],
            inc="se")
        pb7f[0] = ("se", se)
        P["se"] = se

    def q_chain_b(P):
        t, par, se = P["t"], P["par"], P["se"]
        for hp in range(2):
            bA = rot.acquire()
            b.op("tensor", lambda e, hp=hp, par=par, bA=bA: e.matmul(
                bA["ap"][:, :], sel4(2 * hp),
                qrecb[0:4, par * 512:(par + 1) * 512],
                start=True, stop=True, skip_group_check=True),
                waits=[("se", se)] + Pool.w(bA))
            bB = rot.acquire()
            b.op("tensor", lambda e, hp=hp, par=par, bB=bB: e.matmul(
                bB["ap"][:, :], sel4(2 * hp + 1),
                qrecb[0:4, par * 512:(par + 1) * 512],
                start=True, stop=True, skip_group_check=True), waits=Pool.w(bB),
                inc="te")
            tqb = b.cnt["te"]
            for cc in range(4):
                c = hp * 4 + cc
                bank = bA if cc < 2 else bB
                b.op("vector", lambda e, c=c, t=t, bank=bank: e.tensor_mul(
                    Q[:, c * NOWN + t * NT: c * NOWN + (t + 1) * NT],
                    Q[:, c * NOWN + t * NT: c * NOWN + (t + 1) * NT],
                    bank["ap"][:, :]), waits=[("te", tqb)], inc="ve")
            Pool.rel(bA, "ve", b.cnt["ve"])
            Pool.rel(bB, "ve", b.cnt["ve"])
        qb_free[par] = b.cnt["te"]

    for t in range(4):
        q_ready = []
        for c in range(CH):
            if c == 1 and pend[0] is not None:
                q_chain_a(pend[0])
            elif c == 3 and pend[0] is not None:
                q_chain_b(pend[0])
                pend[0] = None
            bq = rot.acquire()
            for kc in range(CH):
                w = (Pool.w(bq) + [gq]) if kc == 0 else []
                b.op("tensor", lambda e, kc=kc, c=c, t=t, bq=bq: e.matmul(
                    bq["ap"][:, :],
                    W0[:, kc * 1024 + c * 128: kc * 1024 + (c + 1) * 128],
                    h1[:, kc * N + t * NT: kc * N + (t + 1) * NT],
                    start=(kc == 0), stop=(kc == 7)), waits=w,
                    inc="te" if kc == 7 else None)
            teq = b.cnt["te"]
            se = b.op("scalar", lambda e, c=c, t=t, bq=bq: e.activation(
                Q[:, c * NOWN + t * NT: c * NOWN + (t + 1) * NT],
                bq["ap"][:, :], AF.Exp, scale=0.25),
                waits=[("te", teq), ("ve", st["xfree"][c // 4])], inc="se")
            Pool.rel(bq, "se", se)
            q_ready.append(se)
        for c in range(CH):
            b.op("tensor", lambda e, c=c, t=t: e.matmul(
                pbs[7][0:4, :], qind(c),
                Q[:, c * NOWN + t * NT: c * NOWN + (t + 1) * NT],
                start=(c == 0), stop=(c == 7), skip_group_check=True),
                waits=[("se", q_ready[c])] + ([pb7f[0]] if c == 0 else []),
                inc="te" if c == 7 else None)
        pend[0] = {"t": t, "par": t % 2, "teqs": b.cnt["te"]}

    woslot = ebf0
    go = load_w(woslot, wo_d, D, CH, waits=Pool.w(ev.items[0]))
    b.op("scalar", lambda e: e.activation(tmp0[:, 0:8], pbs[4][:, 0:8], AF.Ln),
         waits=[("te", kv_te), ("ve", p0_tmp_done)])
    sei = b.op("scalar", lambda e: e.activation(invSpp[:, :], tmp0[:, 0:8],
                                                AF.Exp, scale=-1.0), inc="se")
    cev = []
    for h in range(H):
        ve = b.op("scalar", lambda e, h=h: e.activation(
            Csb[:, h * 512:(h + 1) * 512], pbs[h][:, :], AF.Identity),
            waits=[("te", kv_te), ("te", p0_sbb_done)], inc="se")
        cev.append(ve)
    q_chain_a(pend[0])
    rot2 = mkpool([0, 1])
    for h in range(H):
        for dc in range(2):
            if h == 0 and dc == 1 and pend[0] is not None:
                q_chain_b(pend[0])
                pend[0] = None
            gc = 2 * h + dc
            for jh in range(2):
                bk = rot2.acquire()
                for ec in range(2):
                    w = (Pool.w(bk) + [("se", cev[H - 1]), go]) if ec == 0 else []
                    b.op("tensor", lambda e, h=h, dc=dc, jh=jh, ec=ec, bk=bk: e.matmul(
                        bk["ap"][:, :],
                        Csb[:, h * 512 + ec * 256 + dc * 128:
                            h * 512 + ec * 256 + (dc + 1) * 128],
                        woslot[:, (2 * h + ec) * 1024 + jh * 512:
                               (2 * h + ec) * 1024 + (jh + 1) * 512],
                        start=(ec == 0), stop=(ec == 1)), waits=w,
                        inc="te" if ec == 1 else None)
                tcw = b.cnt["te"]
                secw = b.op("scalar", lambda e, gc=gc, jh=jh, bk=bk: e.activation(
                    CWt[:, gc * 1024 + jh * 512: gc * 1024 + (jh + 1) * 512],
                    bk["ap"][:, :], AF.Identity, scale=invSpp[:, gc:gc + 1]),
                    waits=[("te", tcw), ("se", sei)], inc="se")
                Pool.rel(bk, "se", secw)
    qnorm_all = b.cnt["ve"]
    b.end()

    def tap(name, src_handle, shape, dtype):
        th = nc.dram_tensor(name, shape, dtype, kind="ExternalOutput")
        b.begin("tap")
        ov = syQ.dma(lambda e: e.dma_start(th.ap(), src_handle[:, :]))
        b.op("sync", lambda e, ov=ov: e.wait_ge(b.sem[ov[0]], ov[1]))
        b.end()

    if debug:
        tap("tap_h1", h1, [128, CH * N], BF16)
        dbg["h1"] = "tap_h1"
        tap("tap_q", Q, [128, CH * NOWN], BF16)
        dbg["Qn"] = "tap_q"
        tap("tap_cw", CWt, [128, CH * D], BF16)
        dbg["CW"] = "tap_cw"
    if phases <= 3:
        return (b, dbg) if debug else b

    b.begin("PB")

    def mk_ln(t, xsrc_c, kind, extra_stat_waits=(), cast_eng="scalar"):
        odd = (t % 2 == 1) and (kind in ("ln2", "ln4") or (kind == "ln3" and t == 1))
        lnS = bst[4] if odd else bst[6]
        lnB_ = bst[5] if odd else bst[7]
        ch = {"lnS": lnS, "lnB": lnB_}

        cast_sem = "gp2" if cast_eng == "gpsimd" else "se"

        def stats_c(c, xw=()):
            xb = sbb[:, (c % 2) * 512:(c % 2) * 512 + 512]
            xq_ = sbb[:, 1024 + (c % 2) * 512:1024 + (c % 2) * 512 + 512]
            vw = (list(extra_stat_waits) + list(xw)
                  + [("te", ln_state[f"sbb_free{c % 2}"])])
            if cast_eng == "p5":
                ce = ("gpsimd" if c % 2 == 0
                      else ("scalar" if c in (1, 5) else "vector"))
            else:
                ce = cast_eng
            if ce == "gpsimd":
                e1 = ("gp2", b.op("gpsimd", lambda e, c=c, xb=xb: e.tensor_copy(
                    xb, xsrc_c(c)), waits=vw, inc="gp2"))
            elif ce == "vector":
                e1 = ("ve", b.op("vector", lambda e, c=c, xb=xb: e.tensor_copy(
                    xb, xsrc_c(c)), waits=vw, inc="ve"))
            else:
                e1 = ("se", b.op("scalar", lambda e, c=c, xb=xb: e.activation(
                    xb, xsrc_c(c), AF.Identity), waits=vw, inc="se"))
            if cast_eng == "p5":
                e2 = ("ve", b.op("vector", lambda e, xb=xb, xq_=xq_: e.tensor_mul(
                    xq_, xb, xb), waits=[e1], inc="ve"))
            elif ce == "gpsimd":
                e2 = ("gp2", b.op("gpsimd", lambda e, xb=xb, xq_=xq_: e.tensor_mul(
                    xq_, xb, xb), inc="gp2"))
            else:
                e2 = ("se", b.op("scalar", lambda e, xb=xb, xq_=xq_: e.activation(
                    xq_, xb, AF.Square), inc="se"))
            b.op("tensor", lambda e, xb=xb, c=c: e.matmul(
                lnS["ap"][0:1, :], onescol, xb, start=(c == 0), stop=False,
                skip_group_check=True),
                waits=[e1] + (Pool.w(lnS) if c == 0 else []))
            te = b.op("tensor", lambda e, xq_=xq_, c=c: e.matmul(
                lnS["ap"][32:33, :], onescol, xq_, start=(c == 0), stop=False,
                skip_group_check=True), waits=[e2], inc="te")
            ln_state[f"sbb_free{c % 2}"] = te
            if c == CH - 1:
                ch["stat_done"] = b.cnt["te"]

        def stats():
            for c in range(CH):
                stats_c(c)

        def rows():
            slot = t % 2
            ch["slot"] = slot
            mu_ap = rowb[0:1, slot * 1024: slot * 1024 + 512]
            rsig_ap = rowb[0:1, slot * 1024 + 512: slot * 1024 + 1024]
            ch["mu_ap"], ch["rsig_ap"] = mu_ap, rsig_ap
            se1 = b.op("scalar", lambda e, mu_ap=mu_ap: e.activation(
                mu_ap, lnS["ap"][0:1, :], AF.Identity, scale=1.0 / 1024),
                waits=[("te", ch["stat_done"]),
                       ("te", ln_state[f"rowb_free{slot}"])])
            ch["se1"] = se1
            b.op("vector", lambda e, mu_ap=mu_ap: e.tensor_mul(
                tmp0[0:1, :], mu_ap, mu_ap),
                waits=[("se", se1), ("se", ln_state["musq_free"]),
                       ln_state["tmp_free0"]])
            ve3 = b.op("vector", lambda e: e.scalar_tensor_tensor(
                lnS["ap"][32:33, :], lnS["ap"][32:33, :], 1.0 / 1024, tmp0[0:1, :],
                ALU.mult, ALU.subtract), inc="ve")
            ch["ve3"] = ve3
            b.op("scalar", lambda e: e.activation(
                lnS["ap"][0:1, :], lnS["ap"][32:33, :], AF.Ln, bias=epsap),
                waits=[("ve", ve3)])
            se2 = b.op("scalar", lambda e, rsig_ap=rsig_ap: e.activation(
                rsig_ap, lnS["ap"][0:1, :], AF.Exp, scale=-0.5), inc="se")
            ln_state["musq_free"] = se2
            ch["se2"] = se2

        def bcs():
            b.op("tensor", lambda e: e.matmul(
                lnB_["ap"][:, :], orow, ch["mu_ap"], start=True, stop=True,
                skip_group_check=True),
                waits=[("se", ch["se1"])] + Pool.w(lnB_), inc="te")
            te = b.op("tensor", lambda e: e.matmul(
                lnS["ap"][:, :], orow, ch["rsig_ap"], start=True, stop=True,
                skip_group_check=True),
                waits=[("se", ch["se2"]), ("ve", ch["ve3"])], inc="te")
            ln_state[f"rowb_free{ch['slot']}"] = te
            ch["bc_done"] = te

        ch["stats"] = stats
        ch["stats_c"] = stats_c
        ch["rows"] = rows
        ch["bcs"] = bcs
        return ch

    g_loads = {0: (load_w(W0, wff1_d, FF, CH, col0=0),
                   load_w(W1, wff2_d, D, CH, row0=0))}
    rot3 = mkpool([0, 1, 2])

    sy_chunk = {}

    def load_x_chunk(t, m, waits=()):
        toff = (t % 2) * 4096
        sy_chunk[(t, m)] = syQ.dma(lambda e, t=t, m=m, toff=toff: e.dma_start(
            ap3(estR, toff + m * NT, [[8192, 128], [1, NT]]),
            ap3(xkv, t * NT + m * 128 * N, [[N, 128], [1, NT]])),
            waits=list(waits))

    x1_ve = [0] * 4
    L2 = []
    h2_ready = [[] for _ in range(4)]
    attn_tile_te = [0] * 4

    def l2_apply(t, clo=0, chi=CH):
        ch = L2[t]
        lnS, lnB_ = ch["lnS"], ch["lnB"]
        for c in range(clo, chi):
            tmp = tmp0 if c % 2 == 0 else tmp1
            b.op("vector", lambda e, c=c, t=t, tmp=tmp: e.tensor_sub(
                tmp[:, :], R1[:, c * NOWN + t * NT: c * NOWN + (t + 1) * NT],
                lnB_["ap"][:, :]),
                waits=[("te", ch["bc_done"]),
                       ln_state[f"tmp_free{c % 2}"]], inc="ve")
            ve = b.op("vector", lambda e, tmp=tmp: e.tensor_mul(
                tmp[:, :], tmp[:, :], lnS["ap"][:, :]), inc="ve")
            if t >= 2 and c % 2 == 1:
                gp = b.op("gpsimd", lambda e, c=c, t=t, tmp=tmp: e.tensor_scalar(
                    h2x[:, c * NOWN + t * NT: c * NOWN + (t + 1) * NT],
                    tmp[:, :], ln2g(c), ln2b(c), ALU.mult, ALU.add),
                    waits=[("ve", ve), ("te", attn_tile_te[t])], inc="gp2")
                ln_state[f"tmp_free{c % 2}"] = ("gp2", gp)
            else:
                se = b.op("scalar", lambda e, c=c, t=t, tmp=tmp: e.activation(
                    h2x[:, c * NOWN + t * NT: c * NOWN + (t + 1) * NT], tmp[:, :],
                    AF.Identity, scale=ln2g(c), bias=ln2b(c)),
                    waits=[("ve", ve), ("te", attn_tile_te[t])], inc="se")
                ln_state[f"tmp_free{c % 2}"] = ("se", se)
        if chi == CH:
            Pool.rel(lnB_, "ve", b.cnt["ve"])
            Pool.rel(lnS, "ve", b.cnt["ve"])
            h2_ready[t] = [("se", b.cnt["se"]), ("gp2", b.cnt["gp2"])]

    def attn_hook(t, m):
        if t == 1 and m == 0:
            L2[0]["stats"]()
        elif t == 1 and m == 4:
            L2[0]["rows"]()
        elif t == 2 and m == 0:
            L2[0]["bcs"]()
        elif t == 2 and m == 4:
            l2_apply(0, 0, 3)
        elif t == 2 and m == 6:
            L2[1]["stats"]()
        elif t == 3 and m == 0:
            l2_apply(0, 3, CH)
        elif t == 3 and m == 2:
            L2[1]["rows"]()
        elif t == 3 and m == 5:
            L2[1]["bcs"]()

    for m in range(CH):
        load_x_chunk(0, m)
    for m in range(CH):
        load_x_chunk(1, m)
    for t in range(4):
        toff = (t % 2) * 4096
        th = estR
        for m in range(CH):
            attn_hook(t, m)
            bk = rot3.acquire()
            for kc in range(CH):
                w = (Pool.w(bk)
                     + ([("ve", qnorm_all)] if t == 0 and m == 0 else [])
                     ) if kc == 0 else []
                b.op("tensor", lambda e, m=m, kc=kc, t=t, bk=bk: e.matmul(
                    bk["ap"][:, :],
                    CWt[:, kc * 1024 + m * 128: kc * 1024 + (m + 1) * 128],
                    Q[:, kc * NOWN + t * NT: kc * NOWN + (t + 1) * NT],
                    start=(kc == 0), stop=(kc == 7)), waits=w,
                    inc="te" if kc == 7 else None)
            attn_te = b.cnt["te"]
            ve = b.op("vector", lambda e, m=m, t=t, th=th, toff=toff, bk=bk:
                      e.scalar_tensor_tensor(
                R1[:, m * NOWN + t * NT: m * NOWN + (t + 1) * NT],
                bk["ap"][:, :], bo_c(m),
                th[:, toff + m * NT: toff + (m + 1) * NT],
                ALU.add, ALU.add),
                waits=[sy_chunk[(t, m)], ("te", attn_te)], inc="ve")
            Pool.rel(bk, "ve", ve)
            if t < 2:
                load_x_chunk(t + 2, m, waits=[("ve", ve)])
        x1_ve[t] = b.cnt["ve"]
        attn_tile_te[t] = b.cnt["te"]
        xsrc = lambda c, t=t: R1[:, c * NOWN + t * NT: c * NOWN + (t + 1) * NT]
        L2.append(mk_ln(t, xsrc, "ln2", extra_stat_waits=[("ve", x1_ve[t])],
                        cast_eng="gpsimd" if t >= 2 else "scalar"))
    attn_all = b.cnt["te"]
    x0_read_ve = b.cnt["ve"]

    wslots = [W0, W1, slotD0, slotD1]
    slot_free = [0, 0, 0, 0]
    g_loads[1] = (load_w(wslots[2], wff1_d, FF, CH, col0=1024,
                         waits=[("te", attn_all)]),
                  load_w(wslots[3], wff2_d, D, CH, row0=1024,
                         waits=[("te", attn_all)]))
    rotA = mkpool([0, 1, 2])
    rotB = mkpool([3, 4, 5])
    g_read_te = [0, 0, 0, 0]
    r1_done = [0, 0, 0, 0]
    shv01 = [0, 0]
    q3b_m_te = [0] * CH
    q2_done = [0, 0]
    q3a_done = [0]
    q3b_done = [0]
    for qq in range(4):
        s1 = wslots[(2 * qq) % 4]
        s2 = wslots[(2 * qq + 1) % 4]
        g1, g2 = g_loads[qq]
        gelu_se = [0, 0, 0, 0]
        for t in range(4):
            for jj in range(CH):
                if qq == 0:
                    if t == 0 and jj == 0:
                        l2_apply(1)
                    elif t == 0 and jj == 2:
                        L2[2]["stats"]()
                    elif t == 0 and jj == 5:
                        L2[2]["rows"]()
                    elif t == 0 and jj == 7:
                        L2[2]["bcs"]()
                    elif t == 1 and jj == 0:
                        l2_apply(2)
                    elif t == 1 and jj == 2:
                        L2[3]["stats"]()
                    elif t == 1 and jj == 5:
                        L2[3]["rows"]()
                    elif t == 1 and jj == 7:
                        L2[3]["bcs"]()
                    elif t == 2 and jj == 0:
                        l2_apply(3)
                bk = rotA.acquire()
                for kc in range(CH):
                    w = (Pool.w(bk) + [g1, ("te", g_read_te[t]),
                                       *h2_ready[t],
                                       ("ve", x0_read_ve)]
                         + ([sy_chunk[(3, 6)], sy_chunk[(3, 7)]]
                            if qq == 0 and t == 0 and jj == 0 else [])
                         ) if kc == 0 else []
                    b.op("tensor", lambda e, jj=jj, kc=kc, t=t, s1=s1, bk=bk: e.matmul(
                        bk["ap"][:, :],
                        s1[:, kc * 1024 + jj * 128: kc * 1024 + (jj + 1) * 128],
                        h2x[:, kc * NOWN + t * NT: kc * NOWN + (t + 1) * NT],
                        start=(kc == 0), stop=(kc == 7)), waits=w,
                        inc="te" if kc == 7 else None)
                tef = b.cnt["te"]
                se = b.op("scalar", lambda e, jj=jj, t=t, qq=qq, bk=bk: e.activation(
                    Gq[:, jj * NOWN + t * NT: jj * NOWN + (t + 1) * NT],
                    bk["ap"][:, :], AF.Gelu_apprx_tanh, bias=bff1_c(qq * 8 + jj)),
                    waits=[("te", tef)], inc="se")
                Pool.rel(bk, "se", se)
            gelu_se[t] = b.cnt["se"]
        slot_free[(2 * qq) % 4] = b.cnt["te"]
        if qq == 3:
            q3a_done[0] = b.cnt["te"]
        nq = qq + 2
        if nq < 4:
            g1n = load_w(wslots[(2 * nq) % 4], wff1_d, FF, CH, col0=nq * 1024,
                         waits=[("te", slot_free[(2 * nq) % 4])])
        for t in range(4):
            for m in range(CH):
                bk = rotB.acquire()
                for jc in range(CH):
                    w = (Pool.w(bk) + [g2, ("se", gelu_se[t])]) \
                        if jc == 0 else []
                    b.op("tensor", lambda e, m=m, jc=jc, t=t, s2=s2, bk=bk: e.matmul(
                        bk["ap"][:, :],
                        s2[:, jc * 1024 + m * 128: jc * 1024 + (m + 1) * 128],
                        Gq[:, jc * NOWN + t * NT: jc * NOWN + (t + 1) * NT],
                        start=(jc == 0), stop=(jc == 7)), waits=w,
                        inc="te" if jc == 7 else None)
                teb = b.cnt["te"]
                if qq == 0:
                    ve = b.op("vector", lambda e, m=m, t=t, bk=bk:
                              e.scalar_tensor_tensor(
                        R1[:, m * NOWN + t * NT: m * NOWN + (t + 1) * NT],
                        R1[:, m * NOWN + t * NT: m * NOWN + (t + 1) * NT],
                        bff2_c(m), bk["ap"][:, :],
                        ALU.add, ALU.add), waits=[("te", teb)], inc="ve")
                else:
                    ve = b.op("vector", lambda e, m=m, t=t, bk=bk: e.tensor_add(
                        R1[:, m * NOWN + t * NT: m * NOWN + (t + 1) * NT],
                        R1[:, m * NOWN + t * NT: m * NOWN + (t + 1) * NT],
                        bk["ap"][:, :]), waits=[("te", teb)], inc="ve")
                Pool.rel(bk, "ve", ve)
                if qq == 3:
                    r1_done[t] = b.cnt["ve"]
                    if t < 2:
                        sh_ = shad if t % 2 == 0 else shad2
                        b.op("scalar", lambda e, m=m, t=t, sh_=sh_: e.activation(
                            sh_[:, m * NT:(m + 1) * NT],
                            R1[:, m * NOWN + t * NT: m * NOWN + (t + 1) * NT],
                            AF.Identity),
                            waits=[("ve", ve), ("te", q3a_done[0])], inc="se")
                        if m == CH - 1:
                            shv01[t] = b.cnt["se"]
                    if t == 3:
                        q3b_m_te[m] = b.cnt["te"]
            g_read_te[t] = b.cnt["te"]
        slot_free[(2 * qq + 1) % 4] = b.cnt["te"]
        if qq == 2:
            q2_done[0] = b.cnt["te"]
            gg0 = load_w(W0, wgT_d, HID, CH, col0=0,
                         waits=[("te", slot_free[0])], q=syQ)
            gu0 = load_w(W1, wuT_d, HID, CH, col0=0,
                         waits=[("te", slot_free[1])], q=syQ)
        if qq == 3:
            q3b_done[0] = b.cnt["te"]
        if nq < 4:
            g_loads[nq] = (g1n, load_w(wslots[(2 * nq + 1) % 4], wff2_d, D, CH,
                                       row0=nq * 1024,
                                       waits=[("te", slot_free[(2 * nq + 1) % 4])]))
    gds = [gpQ.dma(lambda e, q=q: e.dma_start(
        ap3(slotD1, q * 256, [[8192, 128], [1024, CH], [1, 256]]),
        ap3(wdd_d, q * 256, [[D, 128], [128 * D, CH], [1, 256]])),
        waits=[("te", q3b_m_te[2 * q + 1])]) for q in range(4)]

    est = Pool([(estR, 0), (estR, 4096)])
    est.items[0]["free"] = ("te", q3b_done[0])
    est.items[1]["free"] = ("te", q3b_done[0])
    rot5 = mkpool([0, 1, 2])
    L3 = []
    x3s_ready = [0] * 4
    u_evac = [0] * 4

    def l3_apply(t):
        ch = L3[t]
        lnS, lnB_ = ch["lnS"], ch["lnB"]
        sc = b.op("scalar", lambda e: e.activation(
            rsgb[:, :], lnS["ap"][:, :], AF.Identity),
            waits=[("te", ch["bc_done"]), ("gp2", ln_state.get("rsgb_free", 0))],
            inc="se")
        for c in range(CH):
            ve = b.op("vector", lambda e, c=c, t=t: e.tensor_sub(
                R1[:, c * NOWN + t * NT: c * NOWN + (t + 1) * NT],
                R1[:, c * NOWN + t * NT: c * NOWN + (t + 1) * NT],
                lnB_["ap"][:, :]), waits=[("te", ch["bc_done"])], inc="ve")
            gp = b.op("gpsimd", lambda e, c=c, t=t: e.tensor_mul(
                R1[:, c * NOWN + t * NT: c * NOWN + (t + 1) * NT],
                R1[:, c * NOWN + t * NT: c * NOWN + (t + 1) * NT],
                rsgb[:, :]), waits=[("se", sc), ("ve", ve)], inc="gp2")
            b.op("scalar", lambda e, c=c, t=t, gp=gp: e.activation(
                h2x[:, c * NOWN + t * NT: c * NOWN + (t + 1) * NT],
                R1[:, c * NOWN + t * NT: c * NOWN + (t + 1) * NT],
                AF.Identity, scale=S127),
                waits=[("te", q3a_done[0]), ("gp2", gp)], inc="se")
        ln_state["rsgb_free"] = b.cnt["gp2"]
        Pool.rel(lnB_, "ve", b.cnt["ve"])
        Pool.rel(lnS, "se", sc)
        x3s_ready[t] = b.cnt["se"]

    shad_free = [0, 0]
    for t in range(4):
        sh = shad if t % 2 == 0 else shad2
        slot = est.acquire()
        th, toff = slot["ap"]
        syv = load_x(th, toff, t * NT, waits=Pool.w(slot))
        xsrc = lambda c, t=t: R1[:, c * NOWN + t * NT: c * NOWN + (t + 1) * NT]
        L3.append(mk_ln(t, xsrc, "ln3", cast_eng="p5"))
        if t >= 2:
            shv = 0
            for c in range(CH):
                shv = b.op("scalar", lambda e, c=c, t=t, sh=sh: e.activation(
                    sh[:, c * NT:(c + 1) * NT],
                    R1[:, c * NOWN + t * NT: c * NOWN + (t + 1) * NT],
                    AF.Identity),
                    waits=[("te", shad_free[t % 2]), ("ve", r1_done[t])],
                    inc="se")
        else:
            shv = shv01[t]
        if t >= 1:
            L3[t - 1]["rows"]()
        ve_hist = [0] * CH
        for m in range(CH):
            if m == 2 and t >= 1:
                L3[t - 1]["bcs"]()
            bk = rot5.acquire()
            gd = gds[m // 2]
            for kc in range(CH):
                w = (Pool.w(bk) + [gd, ("se", shv)]) if kc == 0 else []
                b.op("tensor", lambda e, m=m, kc=kc, bk=bk, sh=sh: e.matmul(
                    bk["ap"][:, :],
                    slotD1[:, kc * 1024 + m * 128: kc * 1024 + (m + 1) * 128],
                    sh[:, kc * NT:(kc + 1) * NT],
                    start=(kc == 0), stop=(kc == 7)), waits=w,
                    inc="te" if kc == 7 else None)
            ted = b.cnt["te"]
            if m == CH - 1:
                shad_free[t % 2] = ted
            ve = b.op("vector", lambda e, m=m, t=t, th=th, toff=toff, bk=bk:
                      e.scalar_tensor_tensor(
                R1[:, m * NOWN + t * NT: m * NOWN + (t + 1) * NT],
                bk["ap"][:, :], bd_c(m),
                th[:, toff + m * NT: toff + (m + 1) * NT],
                ALU.add, ALU.add), waits=[syv, ("te", ted)], inc="ve")
            Pool.rel(bk, "ve", ve)
            ve_hist[m] = ve
            if t < 3 and m >= 2:
                L3[t]["stats_c"](m - 2, xw=[("ve", ve_hist[m - 2])])
            if m == 4 and t >= 1:
                l3_apply(t - 1)
        u_evac[t] = b.cnt["ve"]
        Pool.rel(slot, "ve", u_evac[t])
        if t < 3:
            L3[t]["stats_c"](6, xw=[("ve", ve_hist[6])])
            L3[t]["stats_c"](7, xw=[("ve", ve_hist[7])])
        else:
            for c in range(CH):
                L3[3]["stats_c"](c, xw=[("ve", u_evac[3])])
    u_all = b.cnt["ve"]
    wd_all = b.cnt["te"]

    gwd0 = load_w(wslotE0, wdT_d, D, CH, row0=0, waits=[("ve", u_all)],
                  q=syQ)

    rotA = mkpool([0, 1, 2])
    rotB = mkpool([3, 4, 5])
    wd_free = [0, 0]
    f_read_te = [0, 0, 0, 0]
    silu_par = 0
    silu_free = 0
    loads6 = {0: (gg0, gu0, gwd0)}
    L4 = [None] * 4
    out_edges = []
    oslot_free = [("te", q3a_done[0]), ("te", q3a_done[0])]

    def l4_apply(t):
        ch = L4[t]
        lnS, lnB_ = ch["lnS"], ch["lnB"]
        xs = xsl[t % 2]
        sc = b.op("scalar", lambda e: e.activation(
            rsgb[:, :], lnS["ap"][:, :], AF.Identity),
            waits=[("te", ch["bc_done"]), ("gp2", ln_state.get("rsgb_free", 0))],
            inc="se")
        gp = 0
        for c in range(CH):
            ve = b.op("vector", lambda e, c=c, t=t: e.tensor_sub(
                R1[:, c * NOWN + t * NT: c * NOWN + (t + 1) * NT],
                R1[:, c * NOWN + t * NT: c * NOWN + (t + 1) * NT],
                lnB_["ap"][:, :]), waits=[("te", ch["bc_done"])], inc="ve")
            gp = b.op("gpsimd", lambda e, c=c, t=t, xs=xs: e.tensor_mul(
                xs[:, c * NT:(c + 1) * NT],
                R1[:, c * NOWN + t * NT: c * NOWN + (t + 1) * NT],
                rsgb[:, :]),
                waits=[("se", sc), ("ve", ve), oslot_free[t % 2],
                       ("te", r2a_done[0])], inc="gp2")
        ln_state["rsgb_free"] = b.cnt["gp2"]
        Pool.rel(lnB_, "ve", b.cnt["ve"])
        Pool.rel(lnS, "se", sc)
        ov = syQ.dma(lambda e, t=t, xs=xs: e.dma_start(
            ap3(yout, t * NT, [[NOWN, 128], [128 * NOWN, CH], [1, NT]]),
            ap3(xs, 0, [[xs.shape[1], 128], [NT, CH], [1, NT]])),
            waits=[("gp2", gp)])
        oslot_free[t % 2] = ov
        out_edges.append(ov)

    r2a_done = [0]
    down_evac = [0] * 4
    for r in range(3):
        gg, gu, gwd = loads6[r]
        wds = wslotE0 if r % 2 == 0 else wslotE1
        fve = [0, 0, 0, 0]
        for t in range(4):
            for hc in range(CH):
                if r == 0 and t == 0 and hc == 4:
                    L3[3]["rows"]()
                    L3[3]["bcs"]()
                if r == 0 and t == 1 and hc == 0:
                    l3_apply(3)
                bA = rotA.acquire()
                for kc in range(CH):
                    w = (Pool.w(bA) + [gg, ("te", f_read_te[t]),
                                       ("se", x3s_ready[t])]) \
                        if kc == 0 else []
                    b.op("tensor", lambda e, hc=hc, kc=kc, t=t, bA=bA: e.matmul(
                        bA["ap"][:, :],
                        W0[:, kc * 1024 + hc * 128: kc * 1024 + (hc + 1) * 128],
                        h2x[:, kc * NOWN + t * NT: kc * NOWN + (t + 1) * NT],
                        start=(kc == 0), stop=(kc == 7)), waits=w,
                        inc="te" if kc == 7 else None)
                teg = b.cnt["te"]
                bB = rotB.acquire()
                for kc in range(CH):
                    w = (Pool.w(bB) + [gu]) if kc == 0 else []
                    b.op("tensor", lambda e, hc=hc, kc=kc, t=t, bB=bB: e.matmul(
                        bB["ap"][:, :],
                        W1[:, kc * 1024 + hc * 128: kc * 1024 + (hc + 1) * 128],
                        h2x[:, kc * NOWN + t * NT: kc * NOWN + (t + 1) * NT],
                        start=(kc == 0), stop=(kc == 7)), waits=w,
                        inc="te" if kc == 7 else None)
                teu = b.cnt["te"]
                sl = slbt[:, silu_par * 512: (silu_par + 1) * 512]
                silu_par ^= 1
                se = b.op("scalar", lambda e, sl=sl, bA=bA: e.activation(
                    sl, bA["ap"][:, :], AF.Silu),
                    waits=[("te", teg), ("ve", silu_free)], inc="se")
                Pool.rel(bA, "se", se)
                ve = b.op("vector", lambda e, hc=hc, t=t, sl=sl, bB=bB:
                          e.scalar_tensor_tensor(
                    Ftn[:, hc * NOWN + t * NT: hc * NOWN + (t + 1) * NT],
                    sl, S127, bB["ap"][:, :], ALU.mult, ALU.mult),
                    waits=[("se", se), ("te", teu), ("te", wd_all)], inc="ve")
                Pool.rel(bB, "ve", ve)
                silu_free = ve
            fve[t] = b.cnt["ve"]
        a_done = b.cnt["te"]
        if r == 2:
            r2a_done[0] = a_done
        if r + 1 < 3:
            ggn = load_w(W0, wgT_d, HID, CH, col0=(r + 1) * 1024, q=syQ,
                         waits=[("te", a_done)])
            gun = load_w(W1, wuT_d, HID, CH, col0=(r + 1) * 1024, q=syQ,
                         waits=[("te", a_done)])
            gwn = load_w(wslotE0 if (r + 1) % 2 == 0 else wslotE1, wdT_d, D, CH, q=syQ,
                         row0=(r + 1) * 1024,
                         waits=[("te", wd_free[(r + 1) % 2]), ("ve", u_all)])
            loads6[r + 1] = (ggn, gun, gwn)
        for t in range(4):
            if r == 2 and t >= 1:
                L4[t - 1]["rows"]()
            for m in range(CH):
                if r == 2 and t >= 1 and m == 2:
                    L4[t - 1]["bcs"]()
                if r == 2 and t >= 1 and m == 4:
                    l4_apply(t - 1)
                bk = rotA.acquire()
                for hc in range(CH):
                    w = (Pool.w(bk) + [gwd, ("ve", fve[t])]) if hc == 0 else []
                    b.op("tensor", lambda e, m=m, hc=hc, t=t, wds=wds, bk=bk: e.matmul(
                        bk["ap"][:, :],
                        wds[:, hc * 1024 + m * 128: hc * 1024 + (m + 1) * 128],
                        Ftn[:, hc * NOWN + t * NT: hc * NOWN + (t + 1) * NT],
                        start=(hc == 0), stop=(hc == 7)), waits=w,
                        inc="te" if hc == 7 else None)
                ted = b.cnt["te"]
                ve = b.op("vector", lambda e, m=m, t=t, bk=bk: e.tensor_add(
                    R1[:, m * NOWN + t * NT: m * NOWN + (t + 1) * NT],
                    R1[:, m * NOWN + t * NT: m * NOWN + (t + 1) * NT],
                    bk["ap"][:, :]), waits=[("te", ted)], inc="ve")
                Pool.rel(bk, "ve", ve)
            f_read_te[t] = b.cnt["te"]
            if r == 2:
                down_evac[t] = b.cnt["ve"]
                xsrc = lambda c, t=t: R1[:, c * NOWN + t * NT:
                                         c * NOWN + (t + 1) * NT]
                L4[t] = mk_ln(t, xsrc, "ln4",
                              extra_stat_waits=[("ve", down_evac[t])],
                              cast_eng="p5")
                L4[t]["stats"]()
        wd_free[r % 2] = b.cnt["te"]
    L4[3]["rows"]()
    L4[3]["bcs"]()
    l4_apply(3)
    kf = syQ.k
    b.op("sync", lambda e, v=16 * ((kf + 1) // 2): e.wait_ge(b.sem["syA"], v))
    b.op("sync", lambda e, v=16 * (kf // 2): e.wait_ge(b.sem["syB"], v))
    b.end()

    if debug:
        return b, dbg
    return b


def host_inputs(inputs):
    x = np.ascontiguousarray(np.asarray(inputs["L0_attn_x"], np.float32))
    assert x.shape == (4, N, D)

    def pp(v, ch):
        return np.ascontiguousarray(np.asarray(v, np.float32).reshape(ch, 128).T)

    pvec = np.zeros((128, 90), np.float32)
    pvec[:, 88] = EPS
    pvec[:, 89] = 1.0
    pvec[:, 0:8] = pp(inputs["L0_attn_ln1_g"], 8)
    pvec[:, 8:16] = pp(inputs["L0_attn_ln1_b"], 8)
    pvec[:, 16:24] = pp(inputs["L0_attn_ln2_g"], 8)
    pvec[:, 24:32] = pp(inputs["L0_attn_ln2_b"], 8)
    pvec[:, 32:40] = pp(inputs["L0_attn_bo"], 8)
    pvec[:, 40:48] = pp(inputs["L0_attn_b_ff2"], 8)
    pvec[:, 48:56] = pp(inputs["L0_attn_b_d"], 8)
    pvec[:, 56:88] = pp(inputs["L0_attn_b_ff1"], 32)

    cbf = np.zeros((128, 546), np.float32)
    cbf[:, 0] = 1.0
    for c in range(8):
        cbf[:, 2 + 4 * c + c // 2] = 1.0
    for h in range(4):
        cbf[h, 34 + 128 * h: 34 + 128 * (h + 1)] = 1.0
    cbf = cbf.astype(BF)

    shared = dict(
        wq=np.ascontiguousarray(np.asarray(inputs["L0_attn_wq"], np.float32)),
        wk=np.ascontiguousarray(np.asarray(inputs["L0_attn_wk"], np.float32)),
        wv=np.ascontiguousarray(np.asarray(inputs["L0_attn_wv"], np.float32)),
        wo=np.ascontiguousarray(np.asarray(inputs["L0_attn_wo"], np.float32)),
        wff1=np.ascontiguousarray(np.asarray(inputs["L0_attn_w_ff1"], np.float32)),
        wff2=np.ascontiguousarray(np.asarray(inputs["L0_attn_w_ff2"], np.float32)),
        wdd=np.ascontiguousarray(np.asarray(inputs["L0_attn_w_d"], np.float32)),
        wgT=np.ascontiguousarray(np.asarray(inputs["L0_ffn_w_gate"]).astype(BF).T),
        wuT=np.ascontiguousarray(np.asarray(inputs["L0_ffn_w_up"]).astype(BF).T),
        wdT=np.ascontiguousarray(np.asarray(inputs["L0_ffn_w_down"]).astype(BF).T),
        pvec=pvec, cbf=cbf,
    )
    in_maps = []
    for core in range(8):
        bb, half = core // 2, core % 2
        own = x[bb, half * NOWN:(half + 1) * NOWN]
        oth = x[bb, (1 - half) * NOWN:(2 - half) * NOWN]
        m = dict(shared)
        m["xkv"] = np.ascontiguousarray(np.concatenate([own, oth], 0).T)
        in_maps.append(m)
    return in_maps


def kernel(**inputs):
    in_maps = host_inputs(inputs)
    b = build_nc(debug=False)
    res = run_bass_kernel_spmd(b.nc, in_maps, core_ids=list(range(8)))
    out = np.empty((4, N, D), np.float32)
    for core in range(8):
        bb, half = core // 2, core % 2
        y = np.asarray(res.results[core]["y"])
        out[bb, half * NOWN:(half + 1) * NOWN] = y.T
    return out


# version 19
# speedup vs baseline: 1.0548x; 1.0548x over previous
import numpy as np
import ml_dtypes
from contextlib import ExitStack

import concourse.bass as bass
import concourse.mybir as mybir
from concourse.bass_utils import run_bass_kernel_spmd

F32 = mybir.dt.float32
BF16 = mybir.dt.bfloat16
I32 = mybir.dt.int32
AF = mybir.ActivationFunctionType
ALU = mybir.AluOpType
BF = ml_dtypes.bfloat16

D = 1024
N = 4096
NOWN = 2048
H = 4
FF = 4096
HID = 3072
NT = 512
CH = 8
EPS = 1e-5
S127 = float(np.float32(1.0 / 127.0))
ENGINES = ("sync", "tensor", "vector", "scalar", "gpsimd")


def ap3(h, off, dims):
    return bass.AP(h, off, [list(d) for d in dims])


class B:
    def __init__(self):
        self.nc = bass.Bass("TRN2", target_bir_lowering=False)
        self.es = ExitStack()
        self.sem = {}
        self.cnt = {}
        for s in ("te", "ve", "se", "gp2", "syA", "syB", "gpA", "gpB"):
            self.sem[s] = self.es.enter_context(self.nc.semaphore(s))
            self.cnt[s] = 0
        self.q = None
        self.hwm = None

    def begin(self, name="ph"):
        assert self.q is None
        self.q = {e: [] for e in ENGINES}
        self.hwm = {e: {} for e in ENGINES}
        self.phname = name

    def end(self):
        qs, self.q = self.q, None
        with self.nc.named_scope(self.phname):
            with self.nc.Block() as blk:
                for e in ENGINES:
                    if qs[e]:
                        def runner(eng, _ops=tuple(qs[e])):
                            for f in _ops:
                                f(eng)
                        getattr(blk, e)(runner)

    OWN = {"vector": "ve", "scalar": "se", "gpsimd": "gp2"}

    def op(self, eng, fn, waits=(), inc=None, inck=1):
        own = self.OWN.get(eng)
        if own is not None:
            waits = list(waits) + [(own, self.cnt[own])]
            if inc is None:
                inc, inck = own, 1
        w = []
        hw = self.hwm[eng]
        for s, v in waits:
            if not v:
                continue
            if hw.get(s, 0) < v:
                hw[s] = v
                w.append((self.sem[s], v))
        semh = self.sem[inc] if inc else None
        if inc:
            self.cnt[inc] += inck

        def run(engine, _w=tuple(w), _fn=fn, _s=semh, _k=inck):
            for sh, v in _w:
                engine.wait_ge(sh, v)
            inst = _fn(engine)
            if _s is not None:
                inst.then_inc(_s, _k)
        self.q[eng].append(run)
        return self.cnt[inc] if inc else None


class Pool:
    def __init__(self, items):
        self.items = [{"ap": a, "free": None} for a in items]
        self.i = 0

    def acquire(self):
        r = self.items[self.i % len(self.items)]
        self.i += 1
        return r

    @staticmethod
    def rel(r, sem, val):
        r["free"] = (sem, val)

    @staticmethod
    def w(r):
        return [r["free"]] if r["free"] else []


class DmaQ:

    def __init__(self, b, eng, semA, semB):
        self.b, self.eng, self.sems = b, eng, (semA, semB)
        self.k = 0

    def dma(self, fn, waits=()):
        k = self.k
        self.k += 1
        sem = self.sems[k % 2]
        w = list(waits) + ([(sem, 16 * (k // 2))] if k >= 2 else [])
        self.b.op(self.eng, fn, waits=w, inc=sem, inck=16)
        return (sem, 16 * (k // 2 + 1))


def build_nc(debug=False, phases=99):
    b = B()
    nc = b.nc
    dbg = {}

    xkv = nc.dram_tensor("xkv", [D, N], F32, kind="ExternalInput")
    wq_d = nc.dram_tensor("wq", [D, D], F32, kind="ExternalInput")
    wk_d = nc.dram_tensor("wk", [D, D], F32, kind="ExternalInput")
    wv_d = nc.dram_tensor("wv", [D, D], F32, kind="ExternalInput")
    wo_d = nc.dram_tensor("wo", [D, D], F32, kind="ExternalInput")
    wff1_d = nc.dram_tensor("wff1", [D, FF], F32, kind="ExternalInput")
    wff2_d = nc.dram_tensor("wff2", [FF, D], F32, kind="ExternalInput")
    wdd_d = nc.dram_tensor("wdd", [D, D], F32, kind="ExternalInput")
    wgT_d = nc.dram_tensor("wgT", [D, HID], BF16, kind="ExternalInput")
    wuT_d = nc.dram_tensor("wuT", [D, HID], BF16, kind="ExternalInput")
    wdT_d = nc.dram_tensor("wdT", [HID, D], BF16, kind="ExternalInput")
    pvec_d = nc.dram_tensor("pvec", [128, 90], F32, kind="ExternalInput")
    cbf_d = nc.dram_tensor("cbf", [128, 546], BF16, kind="ExternalInput")
    yout = nc.dram_tensor("y", [D, NOWN], F32, kind="ExternalOutput")

    es = b.es

    def sbt(name, shape, dtype):
        return es.enter_context(nc.sbuf_tensor(name, shape, dtype))

    def alias(name, shape, dtype, base, off_bytes=0):
        return nc.alloc_sbuf_tensor_at(name, shape, dtype,
                                       offset=nc.lookup_mloc(base).addr + off_bytes)

    h1 = sbt("h1", [128, CH * N], BF16)
    R1 = alias("R1", [128, CH * NOWN], F32, h1)
    Q = sbt("Q", [128, CH * NOWN], BF16)
    h2x = alias("h2x", [128, CH * NOWN], BF16, Q)
    xq0 = alias("xq0", [128, CH * NT], F32, Q)
    xq1 = alias("xq1", [128, CH * NT], F32, Q, 16384)
    Dr = sbt("Dr", [128, 16384], BF16)
    CWt = alias("CW", [128, CH * D], BF16, Dr)
    slotD0 = alias("slotD0", [128, 8192], BF16, Dr)
    slotD1 = alias("slotD1", [128, 8192], BF16, Dr, 16384)
    shad = alias("shad", [128, CH * NT], BF16, Dr)
    shad2 = alias("shad2", [128, CH * NT], BF16, Dr, 8192)
    Ftn = Dr
    estR = sbt("estR", [128, 8192], F32)
    Gq = alias("Gq", [128, 8 * NOWN], BF16, estR)
    wslotE0 = alias("wslotE0", [128, 8192], BF16, estR)
    wslotE1 = alias("wslotE1", [128, 8192], BF16, estR, 16384)
    ebf0 = alias("ebf0", [128, 8192], BF16, estR)
    ebf1 = alias("ebf1", [128, 8192], BF16, estR, 16384)
    WARE = sbt("WARE", [128, 16384], BF16)
    W0 = alias("W0", [128, 8192], BF16, WARE)
    W1 = alias("W1", [128, 8192], BF16, WARE, 16384)
    sbb = sbt("sbb", [128, 2048], BF16)
    Csb = alias("Csb", [128, 2048], BF16, sbb)
    slbt = sbt("slbt", [128, 1024], BF16)
    tmp0 = sbt("tmp0", [128, NT], F32)
    tmp1 = sbt("tmp1", [128, NT], F32)
    rsgb = alias("rsgb", [128, NT], BF16, tmp1)
    rowb = sbt("rowb", [1, 2048], BF16)
    qrecf = alias("qrecf", [4, 512], F32, tmp0)
    qrecb = alias("qrecb", [4, 1024], BF16, rowb)
    invSpp = sbt("invSpp", [128, 8], F32)
    pv = sbt("pv", [128, 90], F32)
    cb = sbt("cb", [128, 546], BF16)
    pbs = [es.enter_context(nc.psum_tensor(f"pb{i}", [128, 512], F32)) for i in range(8)]
    bst = [{"ap": p, "free": None} for p in pbs]

    def mkpool(ixs):
        pl = Pool.__new__(Pool)
        pl.items = [bst[i] for i in ixs]
        pl.i = 0
        return pl

    onescol = cb[:, 0:1]
    qind = lambda c: cb[:, 2 + 4 * c: 2 + 4 * (c + 1)]
    sel4 = lambda h: cb[0:4, 34 + 128 * h: 34 + 128 * (h + 1)]
    orow = cb[0:1, 34:162]
    ln1g = lambda c: pv[:, 0 + c: 1 + c]
    ln1b = lambda c: pv[:, 8 + c: 9 + c]
    ln2g = lambda c: pv[:, 16 + c: 17 + c]
    ln2b = lambda c: pv[:, 24 + c: 25 + c]
    bo_c = lambda c: pv[:, 32 + c: 33 + c]
    bff2_c = lambda c: pv[:, 40 + c: 41 + c]
    bd_c = lambda c: pv[:, 48 + c: 49 + c]
    bff1_c = lambda j: pv[:, 56 + j: 57 + j]
    epsap = pv[0:1, 88:89]
    onef = pv[0:1, 89:90]

    syQ = DmaQ(b, "sync", "syA", "syB")
    gpQ = DmaQ(b, "gpsimd", "gpA", "gpB")

    def load_x(dst, off, col0, width=NT, waits=()):
        return syQ.dma(lambda e: e.dma_start(
            ap3(dst, off, [[dst.shape[1], 128], [width, CH], [1, width]]),
            ap3(xkv, col0, [[N, 128], [128 * N, CH], [1, width]])), waits=waits)

    def load_w(dst, dram, cols, kc_cnt, row0=0, col0=0, width=1024, waits=(),
               q=None):
        return (q or gpQ).dma(lambda e: e.dma_start(
            ap3(dst, 0, [[dst.shape[1], 128], [width, kc_cnt], [1, width]]),
            ap3(dram, row0 * cols + col0, [[cols, 128], [128 * cols, kc_cnt], [1, width]])),
            waits=waits)

    ln_state = {"sbb_free0": 0, "sbb_free1": 0,
                "tmp_free0": ("se", 0), "tmp_free1": ("se", 0),
                "rowb_free0": 0, "rowb_free1": 0, "rowb_free2": 0, "rowb_free3": 0,
                "musq_free": 0}
    consts_edges = []

    def ln_batch(tiles, stat_pool, bcm_pool, bcr_pool):
        infos = []
        for ti in tiles:
            xw = list(ti.get("x_waits", ()))
            sbk = stat_pool.acquire()
            bank = sbk["ap"]
            xsrc_c = ti["xsrc_c"]
            for c in range(CH):
                xb = sbb[:, (c % 2) * 512:(c % 2) * 512 + 512]
                xq = sbb[:, 1024 + (c % 2) * 512:1024 + (c % 2) * 512 + 512]
                vw = xw + [("te", ln_state[f"sbb_free{c % 2}"])]
                ve1 = b.op("vector", lambda e, c=c, xb=xb, f=xsrc_c: e.tensor_copy(
                    xb, f(c)), waits=vw, inc="ve")
                ve2 = b.op("vector", lambda e, xb=xb, xq=xq: e.tensor_mul(xq, xb, xb),
                           inc="ve")
                b.op("tensor", lambda e, xb=xb, bank=bank, c=c: e.matmul(
                    bank[0:1, :], onescol, xb, start=(c == 0), stop=False,
                    skip_group_check=True),
                    waits=[("ve", ve1)] + (Pool.w(sbk) + consts_edges if c == 0 else []))
                te = b.op("tensor", lambda e, xq=xq, bank=bank, c=c: e.matmul(
                    bank[32:33, :], onescol, xq, start=(c == 0), stop=False,
                    skip_group_check=True), waits=[("ve", ve2)], inc="te")
                ln_state[f"sbb_free{c % 2}"] = te
            infos.append({"sbk": sbk, "stat_done": b.cnt["te"]})
        def rows(i):
            inf = infos[i]
            bank = inf["sbk"]["ap"]
            slot = i % 2
            mu_ap = rowb[0:1, slot * 1024: slot * 1024 + 512]
            rsig_ap = rowb[0:1, slot * 1024 + 512: slot * 1024 + 1024]
            se1 = b.op("scalar", lambda e, bank=bank, mu_ap=mu_ap: e.activation(
                mu_ap, bank[0:1, :], AF.Identity, scale=1.0 / 1024),
                waits=[("te", inf["stat_done"]), ("te", ln_state[f"rowb_free{slot}"])])
            b.op("vector", lambda e, mu_ap=mu_ap: e.tensor_mul(
                tmp0[0:1, :], mu_ap, mu_ap),
                waits=[("se", se1), ("se", ln_state["musq_free"]),
                       ln_state["tmp_free0"]])
            ve3 = b.op("vector", lambda e, bank=bank: e.scalar_tensor_tensor(
                bank[32:33, :], bank[32:33, :], 1.0 / 1024, tmp0[0:1, :],
                ALU.mult, ALU.subtract), inc="ve")
            b.op("scalar", lambda e, bank=bank: e.activation(
                bank[0:1, :], bank[32:33, :], AF.Ln, bias=epsap),
                waits=[("ve", ve3)])
            se2 = b.op("scalar", lambda e, bank=bank, rsig_ap=rsig_ap: e.activation(
                rsig_ap, bank[0:1, :], AF.Exp, scale=-0.5), inc="se")
            ln_state["musq_free"] = se2
            Pool.rel(inf["sbk"], "se", se2)
            inf.update(slot=slot, mu_ap=mu_ap, rsig_ap=rsig_ap, rows_done=se2)
        rows(0)
        for i, ti in enumerate(tiles):
            if i + 1 < len(tiles):
                rows(i + 1)
            inf = infos[i]
            bm = bcm_pool.acquire()
            b.op("tensor", lambda e, bank=bm["ap"], mu_ap=inf["mu_ap"]: e.matmul(
                bank[:, :], orow, mu_ap, start=True, stop=True,
                skip_group_check=True),
                waits=[("se", inf["rows_done"])] + Pool.w(bm))
            br = bcr_pool.acquire()
            b.op("tensor", lambda e, bank=br["ap"], rsig_ap=inf["rsig_ap"]: e.matmul(
                bank[:, :], orow, rsig_ap, start=True, stop=True,
                skip_group_check=True), waits=Pool.w(br), inc="te")
            bc_done = b.cnt["te"]
            ln_state[f"rowb_free{inf['slot']}"] = bc_done
            xsrc_c = ti["xsrc_c"]
            apply_out = ti["apply_out"]
            for c in range(CH):
                tmp = tmp0 if c % 2 == 0 else tmp1
                b.op("vector", lambda e, c=c, tmp=tmp, bank=bm["ap"], f=xsrc_c:
                     e.tensor_sub(tmp[:, :], f(c), bank[:, :]),
                     waits=[("te", bc_done), ln_state[f"tmp_free{c % 2}"]],
                     inc="ve")
                apply_out(c, tmp, br["ap"])
            Pool.rel(bm, "ve", b.cnt["ve"])
            Pool.rel(br, "ve", b.cnt["ve"])

    b.begin("PA")
    consts_edges.append(syQ.dma(lambda e: e.dma_start(pv[:, :], pvec_d.ap())))
    consts_edges.append(syQ.dma(lambda e: e.dma_start(cb[:, :], cbf_d.ap())))
    gk = load_w(W1, wk_d, D, CH)
    gv = load_w(slotD1, wv_d, D, CH)
    gq = load_w(W0, wq_d, D, CH)

    lnb = pbs[7]
    st = {"bank_free": 0, "h1": [0] * 8, "xedge": [None] * 8,
          "xfree": [0, 0], "stat_done": [0] * 8, "rows": [None] * 8}
    st["xedge"][0] = load_x(xq0, 0, 0, waits=())
    st["xedge"][1] = load_x(xq1, 0, NT, waits=())
    xsl = [xq0, xq1]

    def lnA(t):
        xs = xsl[t % 2]
        for c in range(CH):
            xb = sbb[:, (c % 2) * 512:(c % 2) * 512 + 512]
            xq_ = sbb[:, 1024 + (c % 2) * 512:1024 + (c % 2) * 512 + 512]
            vw = [st["xedge"][t], ("te", ln_state[f"sbb_free{c % 2}"])]
            ve1 = b.op("vector", lambda e, c=c, xb=xb, xs=xs: e.tensor_copy(
                xb, xs[:, c * NT:(c + 1) * NT]), waits=vw, inc="ve")
            ve2 = b.op("scalar", lambda e, xb=xb, xq_=xq_: e.activation(
                xq_, xb, AF.Square), waits=[("ve", ve1)], inc="se")
            b.op("tensor", lambda e, xb=xb, c=c: e.matmul(
                lnb[0:1, :], onescol, xb, start=(c == 0), stop=False,
                skip_group_check=True),
                waits=[("ve", ve1)] + ([("ve", st["bank_free"])] + consts_edges
                                       if c == 0 else []))
            te = b.op("tensor", lambda e, xq_=xq_, c=c: e.matmul(
                lnb[32:33, :], onescol, xq_, start=(c == 0), stop=False,
                skip_group_check=True), waits=[("se", ve2)], inc="te")
            ln_state[f"sbb_free{c % 2}"] = te
        st["stat_done"][t] = b.cnt["te"]

    def lnB1(t):
        slot = t % 2
        mu_ap = rowb[0:1, slot * 1024: slot * 1024 + 512]
        rsig_ap = rowb[0:1, slot * 1024 + 512: slot * 1024 + 1024]
        st["mu_ap"], st["rsig_ap"] = mu_ap, rsig_ap
        se1 = b.op("scalar", lambda e, mu_ap=mu_ap: e.activation(
            mu_ap, lnb[0:1, :], AF.Identity, scale=1.0 / 1024),
            waits=[("te", st["stat_done"][t]),
                   ("te", ln_state[f"rowb_free{slot}"])])
        b.op("vector", lambda e, mu_ap=mu_ap: e.tensor_mul(
            tmp0[0:1, :], mu_ap, mu_ap),
            waits=[("se", se1), ("se", ln_state["musq_free"])])
        ve3 = b.op("vector", lambda e: e.scalar_tensor_tensor(
            lnb[32:33, :], lnb[32:33, :], 1.0 / 1024, tmp0[0:1, :],
            ALU.mult, ALU.subtract), inc="ve")
        st["tmprow"] = ve3
        st["ve3"] = ve3
        b.op("scalar", lambda e: e.activation(
            lnb[0:1, :], lnb[32:33, :], AF.Ln, bias=epsap), waits=[("ve", ve3)])
        se2 = b.op("scalar", lambda e, rsig_ap=rsig_ap: e.activation(
            rsig_ap, lnb[0:1, :], AF.Exp, scale=-0.5), inc="se")
        ln_state["musq_free"] = se2
        st["se2"] = se2

    def lnB2(t):
        xs = xsl[t % 2]
        slot = t % 2
        b.op("tensor", lambda e, mu_ap=st["mu_ap"]: e.matmul(
            lnb[:, :], orow, mu_ap, start=True, stop=True,
            skip_group_check=True),
            waits=[("se", st["se2"]), ("ve", st["ve3"])], inc="te")
        mubc = b.cnt["te"]
        ln_state[f"rowb_free{slot}"] = mubc
        for c in range(CH):
            b.op("vector", lambda e, c=c, xs=xs: e.tensor_sub(
                xs[:, c * NT:(c + 1) * NT], xs[:, c * NT:(c + 1) * NT],
                lnb[:, :]), waits=[("te", mubc)], inc="ve")
        st["p1ve"] = b.cnt["ve"]

    def lnB3(t):
        xs = xsl[t % 2]
        slot = t % 2
        b.op("tensor", lambda e, rsig_ap=st["rsig_ap"]: e.matmul(
            lnb[:, :], orow, rsig_ap, start=True, stop=True,
            skip_group_check=True), waits=[("ve", st["p1ve"])], inc="te")
        rsbc = b.cnt["te"]
        ln_state[f"rowb_free{slot}"] = rsbc
        for c in range(CH):
            b.op("vector", lambda e, c=c, t=t, xs=xs: e.scalar_tensor_tensor(
                h1[:, c * N + t * NT: c * N + (t + 1) * NT],
                xs[:, c * NT:(c + 1) * NT], ln1g(c), lnb[:, :],
                ALU.mult, ALU.mult), waits=[("te", rsbc)], inc="ve")
            b.op("vector", lambda e, c=c, t=t: e.tensor_scalar_add(
                h1[:, c * N + t * NT: c * N + (t + 1) * NT],
                h1[:, c * N + t * NT: c * N + (t + 1) * NT],
                ln1b(c)), inc="ve")
        st["bank_free"] = b.cnt["ve"]
        st["xfree"][slot] = b.cnt["ve"]
        st["h1"][t] = b.cnt["ve"]
        if t + 2 < 8:
            st["xedge"][t + 2] = load_x(xsl[t % 2], 0, (t + 2) * NT,
                                        waits=[("ve", st["xfree"][slot])])

    rot = mkpool([5, 6])
    ev = Pool([ebf0, ebf1])

    def kv_ssub(t, s_, eslab, first_tile_s):
        n0 = t * NT + s_ * 128
        res = {}
        for half in range(2):
            bk = rot.acquire()
            for kc in range(CH):
                w = (Pool.w(bk) + [gk, ("ve", st["h1"][t])]) if kc == 0 else []
                b.op("tensor", lambda e, kc=kc, half=half, n0=n0, bk=bk: e.matmul(
                    bk["ap"][:, :],
                    h1[:, kc * N + n0: kc * N + n0 + 128],
                    W1[:, kc * 1024 + half * 512: kc * 1024 + (half + 1) * 512],
                    start=(kc == 0), stop=(kc == 7)), waits=w,
                    inc="te" if kc == 7 else None)
            tek = b.cnt["te"]
            se = b.op("scalar", lambda e, s_=s_, half=half, eslab=eslab, bk=bk:
                      e.activation(
                eslab[:, s_ * 1024 + half * 512: s_ * 1024 + (half + 1) * 512],
                bk["ap"][:, :], AF.Exp, scale=0.25), waits=[("te", tek)], inc="se")
            Pool.rel(bk, "se", se)
            res[f"e{half}"] = se
            bv = rot.acquire()
            for kc in range(CH):
                w = (Pool.w(bv) + [gv]) if kc == 0 else []
                b.op("tensor", lambda e, kc=kc, half=half, n0=n0, bv=bv: e.matmul(
                    bv["ap"][:, :],
                    h1[:, kc * N + n0: kc * N + n0 + 128],
                    slotD1[:, kc * 1024 + half * 512: kc * 1024 + (half + 1) * 512],
                    start=(kc == 0), stop=(kc == 7)), waits=w,
                    inc="te" if kc == 7 else None)
            tev = b.cnt["te"]
            ve = b.op("scalar", lambda e, s_=s_, half=half, eslab=eslab, bv=bv:
                      e.activation(
                eslab[:, 4096 + s_ * 1024 + half * 512:
                      4096 + s_ * 1024 + (half + 1) * 512],
                bv["ap"][:, :], AF.Identity), waits=[("te", tev)], inc="se")
            Pool.rel(bv, "se", ve)
            res[f"v{half}"] = ve
        first = first_tile_s
        for c in range(CH):
            b.op("tensor", lambda e, s_=s_, c=c, eslab=eslab, first=first: e.matmul(
                pbs[4][:, c:c + 1],
                eslab[:, s_ * 1024 + c * 128: s_ * 1024 + (c + 1) * 128],
                onescol,
                start=first, stop=False, skip_group_check=True),
                waits=[("se", res[f"e{c // 4}"])] + (consts_edges if first else []))
        for h in range(H):
            for ec in range(2):
                b.op("tensor", lambda e, s_=s_, h=h, ec=ec, eslab=eslab,
                     first=first: e.matmul(
                    pbs[h][:, ec * 256: (ec + 1) * 256],
                    eslab[:, 4096 + s_ * 1024 + h * 256 + ec * 128:
                          4096 + s_ * 1024 + h * 256 + (ec + 1) * 128],
                    eslab[:, s_ * 1024 + h * 256: s_ * 1024 + (h + 1) * 256],
                    start=first, stop=False, skip_group_check=True),
                    waits=[("se", res["v1"])],
                    inc="te" if (h == H - 1 and ec == 1) else None)

    lnA(0)
    lnB1(0)
    lnB2(0)
    lnB3(0)
    lnA(1)
    lnB1(1)
    lnB2(1)
    lnB3(1)
    for t in range(8):
        slot = ev.acquire()
        eslab = slot["ap"]
        kv_ssub(t, 0, eslab, t == 0)
        if t + 2 < 8:
            lnA(t + 2)
        kv_ssub(t, 1, eslab, False)
        if t + 2 < 8:
            lnB1(t + 2)
        kv_ssub(t, 2, eslab, False)
        if t + 2 < 8:
            lnB2(t + 2)
        kv_ssub(t, 3, eslab, False)
        if t + 2 < 8:
            lnB3(t + 2)
        Pool.rel(slot, "te", b.cnt["te"])
    kv_te = b.cnt["te"]
    p0_rows_done = max(ln_state["rowb_free0"], ln_state["rowb_free1"])
    p0_tmp_done = st["tmprow"]
    p0_sbb_done = max(ln_state["sbb_free0"], ln_state["sbb_free1"])

    qb_free = [0, 0]
    pb7f = [("ve", st["bank_free"])]
    pend = [None]

    def q_chain_a(P):
        par = P["par"]
        b.op("scalar", lambda e: e.activation(
            qrecf[0:4, :], pbs[7][0:4, :], AF.Ln),
            waits=[("te", P["teqs"]), ("ve", p0_tmp_done)])
        se = b.op("scalar", lambda e, par=par: e.activation(
            qrecb[0:4, par * 512:(par + 1) * 512], qrecf[0:4, :], AF.Exp,
            scale=-1.0), waits=[("te", p0_rows_done), ("te", qb_free[par])],
            inc="se")
        pb7f[0] = ("se", se)
        P["se"] = se

    def q_chain_b(P):
        t, par, se = P["t"], P["par"], P["se"]
        for hp in range(2):
            bA = rot.acquire()
            b.op("tensor", lambda e, hp=hp, par=par, bA=bA: e.matmul(
                bA["ap"][:, :], sel4(2 * hp),
                qrecb[0:4, par * 512:(par + 1) * 512],
                start=True, stop=True, skip_group_check=True),
                waits=[("se", se)] + Pool.w(bA))
            bB = rot.acquire()
            b.op("tensor", lambda e, hp=hp, par=par, bB=bB: e.matmul(
                bB["ap"][:, :], sel4(2 * hp + 1),
                qrecb[0:4, par * 512:(par + 1) * 512],
                start=True, stop=True, skip_group_check=True), waits=Pool.w(bB),
                inc="te")
            tqb = b.cnt["te"]
            for cc in range(4):
                c = hp * 4 + cc
                bank = bA if cc < 2 else bB
                b.op("vector", lambda e, c=c, t=t, bank=bank: e.tensor_mul(
                    Q[:, c * NOWN + t * NT: c * NOWN + (t + 1) * NT],
                    Q[:, c * NOWN + t * NT: c * NOWN + (t + 1) * NT],
                    bank["ap"][:, :]), waits=[("te", tqb)], inc="ve")
            Pool.rel(bA, "ve", b.cnt["ve"])
            Pool.rel(bB, "ve", b.cnt["ve"])
        qb_free[par] = b.cnt["te"]

    for t in range(4):
        q_ready = []
        for c in range(CH):
            if c == 1 and pend[0] is not None:
                q_chain_a(pend[0])
            elif c == 3 and pend[0] is not None:
                q_chain_b(pend[0])
                pend[0] = None
            bq = rot.acquire()
            for kc in range(CH):
                w = (Pool.w(bq) + [gq]) if kc == 0 else []
                b.op("tensor", lambda e, kc=kc, c=c, t=t, bq=bq: e.matmul(
                    bq["ap"][:, :],
                    W0[:, kc * 1024 + c * 128: kc * 1024 + (c + 1) * 128],
                    h1[:, kc * N + t * NT: kc * N + (t + 1) * NT],
                    start=(kc == 0), stop=(kc == 7)), waits=w,
                    inc="te" if kc == 7 else None)
            teq = b.cnt["te"]
            se = b.op("scalar", lambda e, c=c, t=t, bq=bq: e.activation(
                Q[:, c * NOWN + t * NT: c * NOWN + (t + 1) * NT],
                bq["ap"][:, :], AF.Exp, scale=0.25),
                waits=[("te", teq), ("ve", st["xfree"][c // 4])], inc="se")
            Pool.rel(bq, "se", se)
            q_ready.append(se)
        for c in range(CH):
            b.op("tensor", lambda e, c=c, t=t: e.matmul(
                pbs[7][0:4, :], qind(c),
                Q[:, c * NOWN + t * NT: c * NOWN + (t + 1) * NT],
                start=(c == 0), stop=(c == 7), skip_group_check=True),
                waits=[("se", q_ready[c])] + ([pb7f[0]] if c == 0 else []),
                inc="te" if c == 7 else None)
        pend[0] = {"t": t, "par": t % 2, "teqs": b.cnt["te"]}

    woslot = ebf0
    go = load_w(woslot, wo_d, D, CH, waits=Pool.w(ev.items[0]))
    b.op("scalar", lambda e: e.activation(tmp0[:, 0:8], pbs[4][:, 0:8], AF.Ln),
         waits=[("te", kv_te), ("ve", p0_tmp_done)])
    sei = b.op("scalar", lambda e: e.activation(invSpp[:, :], tmp0[:, 0:8],
                                                AF.Exp, scale=-1.0), inc="se")
    cev = []
    for h in range(H):
        ve = b.op("scalar", lambda e, h=h: e.activation(
            Csb[:, h * 512:(h + 1) * 512], pbs[h][:, :], AF.Identity),
            waits=[("te", kv_te), ("te", p0_sbb_done)], inc="se")
        cev.append(ve)
    q_chain_a(pend[0])
    rot2 = mkpool([0, 1])
    for h in range(H):
        for dc in range(2):
            if h == 0 and dc == 1 and pend[0] is not None:
                q_chain_b(pend[0])
                pend[0] = None
            gc = 2 * h + dc
            for jh in range(2):
                bk = rot2.acquire()
                for ec in range(2):
                    w = (Pool.w(bk) + [("se", cev[H - 1]), go]) if ec == 0 else []
                    b.op("tensor", lambda e, h=h, dc=dc, jh=jh, ec=ec, bk=bk: e.matmul(
                        bk["ap"][:, :],
                        Csb[:, h * 512 + ec * 256 + dc * 128:
                            h * 512 + ec * 256 + (dc + 1) * 128],
                        woslot[:, (2 * h + ec) * 1024 + jh * 512:
                               (2 * h + ec) * 1024 + (jh + 1) * 512],
                        start=(ec == 0), stop=(ec == 1)), waits=w,
                        inc="te" if ec == 1 else None)
                tcw = b.cnt["te"]
                secw = b.op("scalar", lambda e, gc=gc, jh=jh, bk=bk: e.activation(
                    CWt[:, gc * 1024 + jh * 512: gc * 1024 + (jh + 1) * 512],
                    bk["ap"][:, :], AF.Identity, scale=invSpp[:, gc:gc + 1]),
                    waits=[("te", tcw), ("se", sei)], inc="se")
                Pool.rel(bk, "se", secw)
    qnorm_all = b.cnt["ve"]
    b.end()

    def tap(name, src_handle, shape, dtype):
        th = nc.dram_tensor(name, shape, dtype, kind="ExternalOutput")
        b.begin("tap")
        ov = syQ.dma(lambda e: e.dma_start(th.ap(), src_handle[:, :]))
        b.op("sync", lambda e, ov=ov: e.wait_ge(b.sem[ov[0]], ov[1]))
        b.end()

    if debug:
        tap("tap_h1", h1, [128, CH * N], BF16)
        dbg["h1"] = "tap_h1"
        tap("tap_q", Q, [128, CH * NOWN], BF16)
        dbg["Qn"] = "tap_q"
        tap("tap_cw", CWt, [128, CH * D], BF16)
        dbg["CW"] = "tap_cw"
    if phases <= 3:
        return (b, dbg) if debug else b

    b.begin("PB")

    def mk_ln(t, xsrc_c, kind, extra_stat_waits=(), cast_eng="scalar"):
        odd = (t % 2 == 1) and (kind in ("ln2", "ln4") or (kind == "ln3" and t == 1))
        lnS = bst[4] if odd else bst[6]
        lnB_ = bst[5] if odd else bst[7]
        ch = {"lnS": lnS, "lnB": lnB_}

        cast_sem = "gp2" if cast_eng == "gpsimd" else "se"

        def stats_c(c, xw=()):
            xb = sbb[:, (c % 2) * 512:(c % 2) * 512 + 512]
            xq_ = sbb[:, 1024 + (c % 2) * 512:1024 + (c % 2) * 512 + 512]
            vw = (list(extra_stat_waits) + list(xw)
                  + [("te", ln_state[f"sbb_free{c % 2}"])])
            if cast_eng == "p5":
                ce = ("gpsimd" if c % 2 == 0
                      else ("scalar" if c in (1, 5) else "vector"))
            else:
                ce = cast_eng
            if ce == "gpsimd":
                e1 = ("gp2", b.op("gpsimd", lambda e, c=c, xb=xb: e.tensor_copy(
                    xb, xsrc_c(c)), waits=vw, inc="gp2"))
            elif ce == "vector":
                e1 = ("ve", b.op("vector", lambda e, c=c, xb=xb: e.tensor_copy(
                    xb, xsrc_c(c)), waits=vw, inc="ve"))
            else:
                e1 = ("se", b.op("scalar", lambda e, c=c, xb=xb: e.activation(
                    xb, xsrc_c(c), AF.Identity), waits=vw, inc="se"))
            if cast_eng == "p5":
                e2 = ("ve", b.op("vector", lambda e, xb=xb, xq_=xq_: e.tensor_mul(
                    xq_, xb, xb), waits=[e1], inc="ve"))
            elif ce == "gpsimd":
                e2 = ("gp2", b.op("gpsimd", lambda e, xb=xb, xq_=xq_: e.tensor_mul(
                    xq_, xb, xb), inc="gp2"))
            else:
                e2 = ("se", b.op("scalar", lambda e, xb=xb, xq_=xq_: e.activation(
                    xq_, xb, AF.Square), inc="se"))
            b.op("tensor", lambda e, xb=xb, c=c: e.matmul(
                lnS["ap"][0:1, :], onescol, xb, start=(c == 0), stop=False,
                skip_group_check=True),
                waits=[e1] + (Pool.w(lnS) if c == 0 else []))
            te = b.op("tensor", lambda e, xq_=xq_, c=c: e.matmul(
                lnS["ap"][32:33, :], onescol, xq_, start=(c == 0), stop=False,
                skip_group_check=True), waits=[e2], inc="te")
            ln_state[f"sbb_free{c % 2}"] = te
            if c == CH - 1:
                ch["stat_done"] = b.cnt["te"]

        def stats():
            for c in range(CH):
                stats_c(c)

        def rows():
            slot = t % 2
            ch["slot"] = slot
            mu_ap = rowb[0:1, slot * 1024: slot * 1024 + 512]
            rsig_ap = rowb[0:1, slot * 1024 + 512: slot * 1024 + 1024]
            ch["mu_ap"], ch["rsig_ap"] = mu_ap, rsig_ap
            se1 = b.op("scalar", lambda e, mu_ap=mu_ap: e.activation(
                mu_ap, lnS["ap"][0:1, :], AF.Identity, scale=1.0 / 1024),
                waits=[("te", ch["stat_done"]),
                       ("te", ln_state[f"rowb_free{slot}"])])
            ch["se1"] = se1
            b.op("vector", lambda e, mu_ap=mu_ap: e.tensor_mul(
                tmp0[0:1, :], mu_ap, mu_ap),
                waits=[("se", se1), ("se", ln_state["musq_free"]),
                       ln_state["tmp_free0"]])
            ve3 = b.op("vector", lambda e: e.scalar_tensor_tensor(
                lnS["ap"][32:33, :], lnS["ap"][32:33, :], 1.0 / 1024, tmp0[0:1, :],
                ALU.mult, ALU.subtract), inc="ve")
            ch["ve3"] = ve3
            b.op("scalar", lambda e: e.activation(
                lnS["ap"][0:1, :], lnS["ap"][32:33, :], AF.Ln, bias=epsap),
                waits=[("ve", ve3)])
            se2 = b.op("scalar", lambda e, rsig_ap=rsig_ap: e.activation(
                rsig_ap, lnS["ap"][0:1, :], AF.Exp, scale=-0.5), inc="se")
            ln_state["musq_free"] = se2
            ch["se2"] = se2

        def bcs():
            b.op("tensor", lambda e: e.matmul(
                lnB_["ap"][:, :], orow, ch["mu_ap"], start=True, stop=True,
                skip_group_check=True),
                waits=[("se", ch["se1"])] + Pool.w(lnB_), inc="te")
            te = b.op("tensor", lambda e: e.matmul(
                lnS["ap"][:, :], orow, ch["rsig_ap"], start=True, stop=True,
                skip_group_check=True),
                waits=[("se", ch["se2"]), ("ve", ch["ve3"])], inc="te")
            ln_state[f"rowb_free{ch['slot']}"] = te
            ch["bc_done"] = te

        ch["stats"] = stats
        ch["stats_c"] = stats_c
        ch["rows"] = rows
        ch["bcs"] = bcs
        return ch

    g_loads = {0: (load_w(W0, wff1_d, FF, CH, col0=0),
                   load_w(W1, wff2_d, D, CH, row0=0))}
    rot3 = mkpool([0, 1, 2])

    sy_chunk = {}

    def load_x_chunk(t, m, waits=()):
        toff = (t % 2) * 4096
        sy_chunk[(t, m)] = syQ.dma(lambda e, t=t, m=m, toff=toff: e.dma_start(
            ap3(estR, toff + m * NT, [[8192, 128], [1, NT]]),
            ap3(xkv, t * NT + m * 128 * N, [[N, 128], [1, NT]])),
            waits=list(waits))

    x1_ve = [0] * 4
    L2 = []
    h2_ready = [[] for _ in range(4)]
    attn_tile_te = [0] * 4

    def l2_apply(t, clo=0, chi=CH):
        ch = L2[t]
        lnS, lnB_ = ch["lnS"], ch["lnB"]
        for c in range(clo, chi):
            tmp = tmp0 if c % 2 == 0 else tmp1
            b.op("vector", lambda e, c=c, t=t, tmp=tmp: e.tensor_sub(
                tmp[:, :], R1[:, c * NOWN + t * NT: c * NOWN + (t + 1) * NT],
                lnB_["ap"][:, :]),
                waits=[("te", ch["bc_done"]),
                       ln_state[f"tmp_free{c % 2}"]], inc="ve")
            ve = b.op("vector", lambda e, tmp=tmp: e.tensor_mul(
                tmp[:, :], tmp[:, :], lnS["ap"][:, :]), inc="ve")
            if t >= 2 and c % 2 == 1:
                gp = b.op("gpsimd", lambda e, c=c, t=t, tmp=tmp: e.tensor_scalar(
                    h2x[:, c * NOWN + t * NT: c * NOWN + (t + 1) * NT],
                    tmp[:, :], ln2g(c), ln2b(c), ALU.mult, ALU.add),
                    waits=[("ve", ve), ("te", attn_tile_te[t])], inc="gp2")
                ln_state[f"tmp_free{c % 2}"] = ("gp2", gp)
            else:
                se = b.op("scalar", lambda e, c=c, t=t, tmp=tmp: e.activation(
                    h2x[:, c * NOWN + t * NT: c * NOWN + (t + 1) * NT], tmp[:, :],
                    AF.Identity, scale=ln2g(c), bias=ln2b(c)),
                    waits=[("ve", ve), ("te", attn_tile_te[t])], inc="se")
                ln_state[f"tmp_free{c % 2}"] = ("se", se)
        if chi == CH:
            Pool.rel(lnB_, "ve", b.cnt["ve"])
            Pool.rel(lnS, "ve", b.cnt["ve"])
            h2_ready[t] = [("se", b.cnt["se"]), ("gp2", b.cnt["gp2"])]

    def attn_hook(t, m):
        if t == 1 and m == 0:
            L2[0]["stats"]()
        elif t == 1 and m == 4:
            L2[0]["rows"]()
        elif t == 2 and m == 0:
            L2[0]["bcs"]()
        elif t == 2 and m == 4:
            l2_apply(0, 0, 3)
        elif t == 2 and m == 6:
            L2[1]["stats"]()
        elif t == 3 and m == 0:
            l2_apply(0, 3, CH)
        elif t == 3 and m == 2:
            L2[1]["rows"]()
        elif t == 3 and m == 5:
            L2[1]["bcs"]()

    for m in range(CH):
        load_x_chunk(0, m)
    for m in range(CH):
        load_x_chunk(1, m)
    for t in range(4):
        toff = (t % 2) * 4096
        th = estR
        for m in range(CH):
            attn_hook(t, m)
            bk = rot3.acquire()
            for kc in range(CH):
                w = (Pool.w(bk)
                     + ([("ve", qnorm_all)] if t == 0 and m == 0 else [])
                     ) if kc == 0 else []
                b.op("tensor", lambda e, m=m, kc=kc, t=t, bk=bk: e.matmul(
                    bk["ap"][:, :],
                    CWt[:, kc * 1024 + m * 128: kc * 1024 + (m + 1) * 128],
                    Q[:, kc * NOWN + t * NT: kc * NOWN + (t + 1) * NT],
                    start=(kc == 0), stop=(kc == 7)), waits=w,
                    inc="te" if kc == 7 else None)
            attn_te = b.cnt["te"]
            ve = b.op("vector", lambda e, m=m, t=t, th=th, toff=toff, bk=bk:
                      e.scalar_tensor_tensor(
                R1[:, m * NOWN + t * NT: m * NOWN + (t + 1) * NT],
                bk["ap"][:, :], bo_c(m),
                th[:, toff + m * NT: toff + (m + 1) * NT],
                ALU.add, ALU.add),
                waits=[sy_chunk[(t, m)], ("te", attn_te)], inc="ve")
            Pool.rel(bk, "ve", ve)
            if t < 2:
                load_x_chunk(t + 2, m, waits=[("ve", ve)])
        x1_ve[t] = b.cnt["ve"]
        attn_tile_te[t] = b.cnt["te"]
        xsrc = lambda c, t=t: R1[:, c * NOWN + t * NT: c * NOWN + (t + 1) * NT]
        L2.append(mk_ln(t, xsrc, "ln2", extra_stat_waits=[("ve", x1_ve[t])],
                        cast_eng="p5" if t >= 2 else "scalar"))
    attn_all = b.cnt["te"]
    x0_read_ve = b.cnt["ve"]

    wslots = [W0, W1, slotD0, slotD1]
    slot_free = [0, 0, 0, 0]
    g_loads[1] = (load_w(wslots[2], wff1_d, FF, CH, col0=1024,
                         waits=[("te", attn_all)]),
                  load_w(wslots[3], wff2_d, D, CH, row0=1024,
                         waits=[("te", attn_all)]))
    rotA = mkpool([0, 1, 2])
    rotB = mkpool([3, 4, 5])
    g_read_te = [0, 0, 0, 0]
    r1_done = [0, 0, 0, 0]
    shv01 = [0, 0]
    q3b_m_te = [0] * CH
    q2_done = [0, 0]
    q3a_done = [0]
    q3b_done = [0]
    for qq in range(4):
        s1 = wslots[(2 * qq) % 4]
        s2 = wslots[(2 * qq + 1) % 4]
        g1, g2 = g_loads[qq]
        gelu_se = [0, 0, 0, 0]
        for t in range(4):
            for jj in range(CH):
                if qq == 0:
                    if t == 0 and jj == 0:
                        l2_apply(1)
                    elif t == 0 and jj == 2:
                        L2[2]["stats"]()
                    elif t == 0 and jj == 5:
                        L2[2]["rows"]()
                    elif t == 0 and jj == 7:
                        L2[2]["bcs"]()
                    elif t == 1 and jj == 0:
                        l2_apply(2)
                    elif t == 1 and jj == 2:
                        L2[3]["stats"]()
                    elif t == 1 and jj == 5:
                        L2[3]["rows"]()
                    elif t == 1 and jj == 7:
                        L2[3]["bcs"]()
                    elif t == 2 and jj == 0:
                        l2_apply(3)
                bk = rotA.acquire()
                for kc in range(CH):
                    w = (Pool.w(bk) + [g1, ("te", g_read_te[t]),
                                       *h2_ready[t],
                                       ("ve", x0_read_ve)]
                         + ([sy_chunk[(3, 6)], sy_chunk[(3, 7)]]
                            if qq == 0 and t == 0 and jj == 0 else [])
                         ) if kc == 0 else []
                    b.op("tensor", lambda e, jj=jj, kc=kc, t=t, s1=s1, bk=bk: e.matmul(
                        bk["ap"][:, :],
                        s1[:, kc * 1024 + jj * 128: kc * 1024 + (jj + 1) * 128],
                        h2x[:, kc * NOWN + t * NT: kc * NOWN + (t + 1) * NT],
                        start=(kc == 0), stop=(kc == 7)), waits=w,
                        inc="te" if kc == 7 else None)
                tef = b.cnt["te"]
                se = b.op("scalar", lambda e, jj=jj, t=t, qq=qq, bk=bk: e.activation(
                    Gq[:, jj * NOWN + t * NT: jj * NOWN + (t + 1) * NT],
                    bk["ap"][:, :], AF.Gelu_apprx_tanh, bias=bff1_c(qq * 8 + jj)),
                    waits=[("te", tef)], inc="se")
                Pool.rel(bk, "se", se)
            gelu_se[t] = b.cnt["se"]
        slot_free[(2 * qq) % 4] = b.cnt["te"]
        if qq == 3:
            q3a_done[0] = b.cnt["te"]
        nq = qq + 2
        if nq < 4:
            g1n = load_w(wslots[(2 * nq) % 4], wff1_d, FF, CH, col0=nq * 1024,
                         waits=[("te", slot_free[(2 * nq) % 4])])
        for t in range(4):
            for m in range(CH):
                bk = rotB.acquire()
                for jc in range(CH):
                    w = (Pool.w(bk) + [g2, ("se", gelu_se[t])]) \
                        if jc == 0 else []
                    b.op("tensor", lambda e, m=m, jc=jc, t=t, s2=s2, bk=bk: e.matmul(
                        bk["ap"][:, :],
                        s2[:, jc * 1024 + m * 128: jc * 1024 + (m + 1) * 128],
                        Gq[:, jc * NOWN + t * NT: jc * NOWN + (t + 1) * NT],
                        start=(jc == 0), stop=(jc == 7)), waits=w,
                        inc="te" if jc == 7 else None)
                teb = b.cnt["te"]
                if qq == 0:
                    ve = b.op("vector", lambda e, m=m, t=t, bk=bk:
                              e.scalar_tensor_tensor(
                        R1[:, m * NOWN + t * NT: m * NOWN + (t + 1) * NT],
                        R1[:, m * NOWN + t * NT: m * NOWN + (t + 1) * NT],
                        bff2_c(m), bk["ap"][:, :],
                        ALU.add, ALU.add), waits=[("te", teb)], inc="ve")
                else:
                    ve = b.op("vector", lambda e, m=m, t=t, bk=bk: e.tensor_add(
                        R1[:, m * NOWN + t * NT: m * NOWN + (t + 1) * NT],
                        R1[:, m * NOWN + t * NT: m * NOWN + (t + 1) * NT],
                        bk["ap"][:, :]), waits=[("te", teb)], inc="ve")
                Pool.rel(bk, "ve", ve)
                if qq == 3:
                    r1_done[t] = b.cnt["ve"]
                    if t < 2:
                        sh_ = shad if t % 2 == 0 else shad2
                        b.op("scalar", lambda e, m=m, t=t, sh_=sh_: e.activation(
                            sh_[:, m * NT:(m + 1) * NT],
                            R1[:, m * NOWN + t * NT: m * NOWN + (t + 1) * NT],
                            AF.Identity),
                            waits=[("ve", ve), ("te", q3a_done[0])], inc="se")
                        if m == CH - 1:
                            shv01[t] = b.cnt["se"]
                    if t == 3:
                        q3b_m_te[m] = b.cnt["te"]
            g_read_te[t] = b.cnt["te"]
        slot_free[(2 * qq + 1) % 4] = b.cnt["te"]
        if qq == 2:
            q2_done[0] = b.cnt["te"]
            gg0 = load_w(W0, wgT_d, HID, CH, col0=0,
                         waits=[("te", slot_free[0])], q=syQ)
            gu0 = load_w(W1, wuT_d, HID, CH, col0=0,
                         waits=[("te", slot_free[1])], q=syQ)
        if qq == 3:
            q3b_done[0] = b.cnt["te"]
        if nq < 4:
            g_loads[nq] = (g1n, load_w(wslots[(2 * nq + 1) % 4], wff2_d, D, CH,
                                       row0=nq * 1024,
                                       waits=[("te", slot_free[(2 * nq + 1) % 4])]))
    gds = [gpQ.dma(lambda e, q=q: e.dma_start(
        ap3(slotD1, q * 256, [[8192, 128], [1024, CH], [1, 256]]),
        ap3(wdd_d, q * 256, [[D, 128], [128 * D, CH], [1, 256]])),
        waits=[("te", q3b_m_te[2 * q + 1])]) for q in range(4)]

    est = Pool([(estR, 0), (estR, 4096)])
    est.items[0]["free"] = ("te", q3b_done[0])
    est.items[1]["free"] = ("te", q3b_done[0])
    rot5 = mkpool([0, 1, 2])
    L3 = []
    x3s_ready = [0] * 4
    u_evac = [0] * 4

    def l3_apply(t):
        ch = L3[t]
        lnS, lnB_ = ch["lnS"], ch["lnB"]
        sc = b.op("scalar", lambda e: e.activation(
            rsgb[:, :], lnS["ap"][:, :], AF.Identity),
            waits=[("te", ch["bc_done"]), ("gp2", ln_state.get("rsgb_free", 0))],
            inc="se")
        for c in range(CH):
            ve = b.op("vector", lambda e, c=c, t=t: e.tensor_sub(
                R1[:, c * NOWN + t * NT: c * NOWN + (t + 1) * NT],
                R1[:, c * NOWN + t * NT: c * NOWN + (t + 1) * NT],
                lnB_["ap"][:, :]), waits=[("te", ch["bc_done"])], inc="ve")
            gp = b.op("gpsimd", lambda e, c=c, t=t: e.tensor_mul(
                R1[:, c * NOWN + t * NT: c * NOWN + (t + 1) * NT],
                R1[:, c * NOWN + t * NT: c * NOWN + (t + 1) * NT],
                rsgb[:, :]), waits=[("se", sc), ("ve", ve)], inc="gp2")
            b.op("scalar", lambda e, c=c, t=t, gp=gp: e.activation(
                h2x[:, c * NOWN + t * NT: c * NOWN + (t + 1) * NT],
                R1[:, c * NOWN + t * NT: c * NOWN + (t + 1) * NT],
                AF.Identity, scale=S127),
                waits=[("te", q3a_done[0]), ("gp2", gp)], inc="se")
        ln_state["rsgb_free"] = b.cnt["gp2"]
        Pool.rel(lnB_, "ve", b.cnt["ve"])
        Pool.rel(lnS, "se", sc)
        x3s_ready[t] = b.cnt["se"]

    shad_free = [0, 0]
    for t in range(4):
        sh = shad if t % 2 == 0 else shad2
        slot = est.acquire()
        th, toff = slot["ap"]
        syv = load_x(th, toff, t * NT, waits=Pool.w(slot))
        xsrc = lambda c, t=t: R1[:, c * NOWN + t * NT: c * NOWN + (t + 1) * NT]
        L3.append(mk_ln(t, xsrc, "ln3", cast_eng="p5"))
        if t >= 2:
            shv = 0
            for c in range(CH):
                shv = b.op("scalar", lambda e, c=c, t=t, sh=sh: e.activation(
                    sh[:, c * NT:(c + 1) * NT],
                    R1[:, c * NOWN + t * NT: c * NOWN + (t + 1) * NT],
                    AF.Identity),
                    waits=[("te", shad_free[t % 2]), ("ve", r1_done[t])],
                    inc="se")
        else:
            shv = shv01[t]
        if t >= 1:
            L3[t - 1]["rows"]()
        ve_hist = [0] * CH
        for m in range(CH):
            if m == 2 and t >= 1:
                L3[t - 1]["bcs"]()
            bk = rot5.acquire()
            gd = gds[m // 2]
            for kc in range(CH):
                w = (Pool.w(bk) + [gd, ("se", shv)]) if kc == 0 else []
                b.op("tensor", lambda e, m=m, kc=kc, bk=bk, sh=sh: e.matmul(
                    bk["ap"][:, :],
                    slotD1[:, kc * 1024 + m * 128: kc * 1024 + (m + 1) * 128],
                    sh[:, kc * NT:(kc + 1) * NT],
                    start=(kc == 0), stop=(kc == 7)), waits=w,
                    inc="te" if kc == 7 else None)
            ted = b.cnt["te"]
            if m == CH - 1:
                shad_free[t % 2] = ted
            ve = b.op("vector", lambda e, m=m, t=t, th=th, toff=toff, bk=bk:
                      e.scalar_tensor_tensor(
                R1[:, m * NOWN + t * NT: m * NOWN + (t + 1) * NT],
                bk["ap"][:, :], bd_c(m),
                th[:, toff + m * NT: toff + (m + 1) * NT],
                ALU.add, ALU.add), waits=[syv, ("te", ted)], inc="ve")
            Pool.rel(bk, "ve", ve)
            ve_hist[m] = ve
            if t < 3 and m >= 2:
                L3[t]["stats_c"](m - 2, xw=[("ve", ve_hist[m - 2])])
            if m == 4 and t >= 1:
                l3_apply(t - 1)
        u_evac[t] = b.cnt["ve"]
        Pool.rel(slot, "ve", u_evac[t])
        if t < 3:
            L3[t]["stats_c"](6, xw=[("ve", ve_hist[6])])
            L3[t]["stats_c"](7, xw=[("ve", ve_hist[7])])
        else:
            for c in range(CH):
                L3[3]["stats_c"](c, xw=[("ve", u_evac[3])])
    u_all = b.cnt["ve"]
    wd_all = b.cnt["te"]

    gwd0 = load_w(wslotE0, wdT_d, D, CH, row0=0, waits=[("ve", u_all)],
                  q=syQ)

    rotA = mkpool([0, 1, 2])
    rotB = mkpool([3, 4, 5])
    wd_free = [0, 0]
    f_read_te = [0, 0, 0, 0]
    silu_par = 0
    silu_free = 0
    loads6 = {0: (gg0, gu0, gwd0)}
    L4 = [None] * 4
    out_edges = []
    oslot_free = [("te", q3a_done[0]), ("te", q3a_done[0])]

    def l4_apply(t):
        ch = L4[t]
        lnS, lnB_ = ch["lnS"], ch["lnB"]
        xs = xsl[t % 2]
        sc = b.op("scalar", lambda e: e.activation(
            rsgb[:, :], lnS["ap"][:, :], AF.Identity),
            waits=[("te", ch["bc_done"]), ("gp2", ln_state.get("rsgb_free", 0))],
            inc="se")
        gp = 0
        for c in range(CH):
            ve = b.op("vector", lambda e, c=c, t=t: e.tensor_sub(
                R1[:, c * NOWN + t * NT: c * NOWN + (t + 1) * NT],
                R1[:, c * NOWN + t * NT: c * NOWN + (t + 1) * NT],
                lnB_["ap"][:, :]), waits=[("te", ch["bc_done"])], inc="ve")
            gp = b.op("gpsimd", lambda e, c=c, t=t, xs=xs: e.tensor_mul(
                xs[:, c * NT:(c + 1) * NT],
                R1[:, c * NOWN + t * NT: c * NOWN + (t + 1) * NT],
                rsgb[:, :]),
                waits=[("se", sc), ("ve", ve), oslot_free[t % 2],
                       ("te", r2a_done[0])], inc="gp2")
        ln_state["rsgb_free"] = b.cnt["gp2"]
        Pool.rel(lnB_, "ve", b.cnt["ve"])
        Pool.rel(lnS, "se", sc)
        ov = syQ.dma(lambda e, t=t, xs=xs: e.dma_start(
            ap3(yout, t * NT, [[NOWN, 128], [128 * NOWN, CH], [1, NT]]),
            ap3(xs, 0, [[xs.shape[1], 128], [NT, CH], [1, NT]])),
            waits=[("gp2", gp)])
        oslot_free[t % 2] = ov
        out_edges.append(ov)

    r2a_done = [0]
    down_evac = [0] * 4
    for r in range(3):
        gg, gu, gwd = loads6[r]
        wds = wslotE0 if r % 2 == 0 else wslotE1
        fve = [0, 0, 0, 0]
        for t in range(4):
            for hc in range(CH):
                if r == 0 and t == 0 and hc == 4:
                    L3[3]["rows"]()
                    L3[3]["bcs"]()
                if r == 0 and t == 1 and hc == 0:
                    l3_apply(3)
                bA = rotA.acquire()
                for kc in range(CH):
                    w = (Pool.w(bA) + [gg, ("te", f_read_te[t]),
                                       ("se", x3s_ready[t])]) \
                        if kc == 0 else []
                    b.op("tensor", lambda e, hc=hc, kc=kc, t=t, bA=bA: e.matmul(
                        bA["ap"][:, :],
                        W0[:, kc * 1024 + hc * 128: kc * 1024 + (hc + 1) * 128],
                        h2x[:, kc * NOWN + t * NT: kc * NOWN + (t + 1) * NT],
                        start=(kc == 0), stop=(kc == 7)), waits=w,
                        inc="te" if kc == 7 else None)
                teg = b.cnt["te"]
                bB = rotB.acquire()
                for kc in range(CH):
                    w = (Pool.w(bB) + [gu]) if kc == 0 else []
                    b.op("tensor", lambda e, hc=hc, kc=kc, t=t, bB=bB: e.matmul(
                        bB["ap"][:, :],
                        W1[:, kc * 1024 + hc * 128: kc * 1024 + (hc + 1) * 128],
                        h2x[:, kc * NOWN + t * NT: kc * NOWN + (t + 1) * NT],
                        start=(kc == 0), stop=(kc == 7)), waits=w,
                        inc="te" if kc == 7 else None)
                teu = b.cnt["te"]
                sl = slbt[:, silu_par * 512: (silu_par + 1) * 512]
                silu_par ^= 1
                se = b.op("scalar", lambda e, sl=sl, bA=bA: e.activation(
                    sl, bA["ap"][:, :], AF.Silu),
                    waits=[("te", teg), ("ve", silu_free)], inc="se")
                Pool.rel(bA, "se", se)
                ve = b.op("vector", lambda e, hc=hc, t=t, sl=sl, bB=bB:
                          e.scalar_tensor_tensor(
                    Ftn[:, hc * NOWN + t * NT: hc * NOWN + (t + 1) * NT],
                    sl, S127, bB["ap"][:, :], ALU.mult, ALU.mult),
                    waits=[("se", se), ("te", teu), ("te", wd_all)], inc="ve")
                Pool.rel(bB, "ve", ve)
                silu_free = ve
            fve[t] = b.cnt["ve"]
        a_done = b.cnt["te"]
        if r == 2:
            r2a_done[0] = a_done
        if r + 1 < 3:
            ggn = load_w(W0, wgT_d, HID, CH, col0=(r + 1) * 1024, q=syQ,
                         waits=[("te", a_done)])
            gun = load_w(W1, wuT_d, HID, CH, col0=(r + 1) * 1024, q=syQ,
                         waits=[("te", a_done)])
            gwn = load_w(wslotE0 if (r + 1) % 2 == 0 else wslotE1, wdT_d, D, CH, q=syQ,
                         row0=(r + 1) * 1024,
                         waits=[("te", wd_free[(r + 1) % 2]), ("ve", u_all)])
            loads6[r + 1] = (ggn, gun, gwn)
        for t in range(4):
            if r == 2 and t >= 1:
                L4[t - 1]["rows"]()
            for m in range(CH):
                if r == 2 and t >= 1 and m == 2:
                    L4[t - 1]["bcs"]()
                if r == 2 and t >= 1 and m == 4:
                    l4_apply(t - 1)
                bk = rotA.acquire()
                for hc in range(CH):
                    w = (Pool.w(bk) + [gwd, ("ve", fve[t])]) if hc == 0 else []
                    b.op("tensor", lambda e, m=m, hc=hc, t=t, wds=wds, bk=bk: e.matmul(
                        bk["ap"][:, :],
                        wds[:, hc * 1024 + m * 128: hc * 1024 + (m + 1) * 128],
                        Ftn[:, hc * NOWN + t * NT: hc * NOWN + (t + 1) * NT],
                        start=(hc == 0), stop=(hc == 7)), waits=w,
                        inc="te" if hc == 7 else None)
                ted = b.cnt["te"]
                ve = b.op("vector", lambda e, m=m, t=t, bk=bk: e.tensor_add(
                    R1[:, m * NOWN + t * NT: m * NOWN + (t + 1) * NT],
                    R1[:, m * NOWN + t * NT: m * NOWN + (t + 1) * NT],
                    bk["ap"][:, :]), waits=[("te", ted)], inc="ve")
                Pool.rel(bk, "ve", ve)
            f_read_te[t] = b.cnt["te"]
            if r == 2:
                down_evac[t] = b.cnt["ve"]
                xsrc = lambda c, t=t: R1[:, c * NOWN + t * NT:
                                         c * NOWN + (t + 1) * NT]
                L4[t] = mk_ln(t, xsrc, "ln4",
                              extra_stat_waits=[("ve", down_evac[t])],
                              cast_eng="p5")
                L4[t]["stats"]()
        wd_free[r % 2] = b.cnt["te"]
    L4[3]["rows"]()
    L4[3]["bcs"]()
    l4_apply(3)
    kf = syQ.k
    b.op("sync", lambda e, v=16 * ((kf + 1) // 2): e.wait_ge(b.sem["syA"], v))
    b.op("sync", lambda e, v=16 * (kf // 2): e.wait_ge(b.sem["syB"], v))
    b.end()

    if debug:
        return b, dbg
    return b


def host_inputs(inputs):
    x = np.ascontiguousarray(np.asarray(inputs["L0_attn_x"], np.float32))
    assert x.shape == (4, N, D)

    def pp(v, ch):
        return np.ascontiguousarray(np.asarray(v, np.float32).reshape(ch, 128).T)

    pvec = np.zeros((128, 90), np.float32)
    pvec[:, 88] = EPS
    pvec[:, 89] = 1.0
    pvec[:, 0:8] = pp(inputs["L0_attn_ln1_g"], 8)
    pvec[:, 8:16] = pp(inputs["L0_attn_ln1_b"], 8)
    pvec[:, 16:24] = pp(inputs["L0_attn_ln2_g"], 8)
    pvec[:, 24:32] = pp(inputs["L0_attn_ln2_b"], 8)
    pvec[:, 32:40] = pp(inputs["L0_attn_bo"], 8)
    pvec[:, 40:48] = pp(inputs["L0_attn_b_ff2"], 8)
    pvec[:, 48:56] = pp(inputs["L0_attn_b_d"], 8)
    pvec[:, 56:88] = pp(inputs["L0_attn_b_ff1"], 32)

    cbf = np.zeros((128, 546), np.float32)
    cbf[:, 0] = 1.0
    for c in range(8):
        cbf[:, 2 + 4 * c + c // 2] = 1.0
    for h in range(4):
        cbf[h, 34 + 128 * h: 34 + 128 * (h + 1)] = 1.0
    cbf = cbf.astype(BF)

    shared = dict(
        wq=np.ascontiguousarray(np.asarray(inputs["L0_attn_wq"], np.float32)),
        wk=np.ascontiguousarray(np.asarray(inputs["L0_attn_wk"], np.float32)),
        wv=np.ascontiguousarray(np.asarray(inputs["L0_attn_wv"], np.float32)),
        wo=np.ascontiguousarray(np.asarray(inputs["L0_attn_wo"], np.float32)),
        wff1=np.ascontiguousarray(np.asarray(inputs["L0_attn_w_ff1"], np.float32)),
        wff2=np.ascontiguousarray(np.asarray(inputs["L0_attn_w_ff2"], np.float32)),
        wdd=np.ascontiguousarray(np.asarray(inputs["L0_attn_w_d"], np.float32)),
        wgT=np.ascontiguousarray(np.asarray(inputs["L0_ffn_w_gate"]).astype(BF).T),
        wuT=np.ascontiguousarray(np.asarray(inputs["L0_ffn_w_up"]).astype(BF).T),
        wdT=np.ascontiguousarray(np.asarray(inputs["L0_ffn_w_down"]).astype(BF).T),
        pvec=pvec, cbf=cbf,
    )
    in_maps = []
    for core in range(8):
        bb, half = core // 2, core % 2
        own = x[bb, half * NOWN:(half + 1) * NOWN]
        oth = x[bb, (1 - half) * NOWN:(2 - half) * NOWN]
        m = dict(shared)
        m["xkv"] = np.ascontiguousarray(np.concatenate([own, oth], 0).T)
        in_maps.append(m)
    return in_maps


def kernel(**inputs):
    in_maps = host_inputs(inputs)
    b = build_nc(debug=False)
    res = run_bass_kernel_spmd(b.nc, in_maps, core_ids=list(range(8)))
    out = np.empty((4, N, D), np.float32)
    for core in range(8):
        bb, half = core // 2, core % 2
        y = np.asarray(res.results[core]["y"])
        out[bb, half * NOWN:(half + 1) * NOWN] = y.T
    return out


# version 20 (final)
# speedup vs baseline: 1.0571x; 1.0022x over previous
import numpy as np
import ml_dtypes
from contextlib import ExitStack

import concourse.bass as bass
import concourse.mybir as mybir
from concourse.bass_utils import run_bass_kernel_spmd

F32 = mybir.dt.float32
BF16 = mybir.dt.bfloat16
I32 = mybir.dt.int32
AF = mybir.ActivationFunctionType
ALU = mybir.AluOpType
BF = ml_dtypes.bfloat16

D = 1024
N = 4096
NOWN = 2048
H = 4
FF = 4096
HID = 3072
NT = 512
CH = 8
EPS = 1e-5
S127 = float(np.float32(1.0 / 127.0))
ENGINES = ("sync", "tensor", "vector", "scalar", "gpsimd")


def ap3(h, off, dims):
    return bass.AP(h, off, [list(d) for d in dims])


class B:
    def __init__(self):
        self.nc = bass.Bass("TRN2", target_bir_lowering=False)
        self.es = ExitStack()
        self.sem = {}
        self.cnt = {}
        for s in ("te", "ve", "se", "gp2", "syA", "syB", "gpA", "gpB"):
            self.sem[s] = self.es.enter_context(self.nc.semaphore(s))
            self.cnt[s] = 0
        self.q = None
        self.hwm = None

    def begin(self, name="ph"):
        assert self.q is None
        self.q = {e: [] for e in ENGINES}
        self.hwm = {e: {} for e in ENGINES}
        self.phname = name

    def end(self):
        qs, self.q = self.q, None
        with self.nc.named_scope(self.phname):
            with self.nc.Block() as blk:
                for e in ENGINES:
                    if qs[e]:
                        def runner(eng, _ops=tuple(qs[e])):
                            for f in _ops:
                                f(eng)
                        getattr(blk, e)(runner)

    OWN = {"vector": "ve", "scalar": "se", "gpsimd": "gp2"}

    def op(self, eng, fn, waits=(), inc=None, inck=1):
        own = self.OWN.get(eng)
        if own is not None:
            waits = list(waits) + [(own, self.cnt[own])]
            if inc is None:
                inc, inck = own, 1
        w = []
        hw = self.hwm[eng]
        for s, v in waits:
            if not v:
                continue
            if hw.get(s, 0) < v:
                hw[s] = v
                w.append((self.sem[s], v))
        semh = self.sem[inc] if inc else None
        if inc:
            self.cnt[inc] += inck

        def run(engine, _w=tuple(w), _fn=fn, _s=semh, _k=inck):
            for sh, v in _w:
                engine.wait_ge(sh, v)
            inst = _fn(engine)
            if _s is not None:
                inst.then_inc(_s, _k)
        self.q[eng].append(run)
        return self.cnt[inc] if inc else None


class Pool:
    def __init__(self, items):
        self.items = [{"ap": a, "free": None} for a in items]
        self.i = 0

    def acquire(self):
        r = self.items[self.i % len(self.items)]
        self.i += 1
        return r

    @staticmethod
    def rel(r, sem, val):
        r["free"] = (sem, val)

    @staticmethod
    def w(r):
        return [r["free"]] if r["free"] else []


class DmaQ:

    def __init__(self, b, eng, semA, semB):
        self.b, self.eng, self.sems = b, eng, (semA, semB)
        self.k = 0

    def dma(self, fn, waits=()):
        k = self.k
        self.k += 1
        sem = self.sems[k % 2]
        w = list(waits) + ([(sem, 16 * (k // 2))] if k >= 2 else [])
        self.b.op(self.eng, fn, waits=w, inc=sem, inck=16)
        return (sem, 16 * (k // 2 + 1))


def build_nc(debug=False, phases=99):
    b = B()
    nc = b.nc
    dbg = {}

    xkv = nc.dram_tensor("xkv", [D, N], F32, kind="ExternalInput")
    wq_d = nc.dram_tensor("wq", [D, D], F32, kind="ExternalInput")
    wk_d = nc.dram_tensor("wk", [D, D], F32, kind="ExternalInput")
    wv_d = nc.dram_tensor("wv", [D, D], F32, kind="ExternalInput")
    wo_d = nc.dram_tensor("wo", [D, D], F32, kind="ExternalInput")
    wff1_d = nc.dram_tensor("wff1", [D, FF], F32, kind="ExternalInput")
    wff2_d = nc.dram_tensor("wff2", [FF, D], F32, kind="ExternalInput")
    wdd_d = nc.dram_tensor("wdd", [D, D], F32, kind="ExternalInput")
    wgT_d = nc.dram_tensor("wgT", [D, HID], BF16, kind="ExternalInput")
    wuT_d = nc.dram_tensor("wuT", [D, HID], BF16, kind="ExternalInput")
    wdT_d = nc.dram_tensor("wdT", [HID, D], BF16, kind="ExternalInput")
    pvec_d = nc.dram_tensor("pvec", [128, 90], F32, kind="ExternalInput")
    cbf_d = nc.dram_tensor("cbf", [128, 546], BF16, kind="ExternalInput")
    yout = nc.dram_tensor("y", [D, NOWN], F32, kind="ExternalOutput")

    es = b.es

    def sbt(name, shape, dtype):
        return es.enter_context(nc.sbuf_tensor(name, shape, dtype))

    def alias(name, shape, dtype, base, off_bytes=0):
        return nc.alloc_sbuf_tensor_at(name, shape, dtype,
                                       offset=nc.lookup_mloc(base).addr + off_bytes)

    h1 = sbt("h1", [128, CH * N], BF16)
    R1 = alias("R1", [128, CH * NOWN], F32, h1)
    Q = sbt("Q", [128, CH * NOWN], BF16)
    h2x = alias("h2x", [128, CH * NOWN], BF16, Q)
    xq0 = alias("xq0", [128, CH * NT], F32, Q)
    xq1 = alias("xq1", [128, CH * NT], F32, Q, 16384)
    Dr = sbt("Dr", [128, 16384], BF16)
    CWt = alias("CW", [128, CH * D], BF16, Dr)
    slotD0 = alias("slotD0", [128, 8192], BF16, Dr)
    slotD1 = alias("slotD1", [128, 8192], BF16, Dr, 16384)
    shad = alias("shad", [128, CH * NT], BF16, Dr)
    shad2 = alias("shad2", [128, CH * NT], BF16, Dr, 8192)
    Ftn = Dr
    estR = sbt("estR", [128, 8192], F32)
    Gq = alias("Gq", [128, 8 * NOWN], BF16, estR)
    wslotE0 = alias("wslotE0", [128, 8192], BF16, estR)
    wslotE1 = alias("wslotE1", [128, 8192], BF16, estR, 16384)
    ebf0 = alias("ebf0", [128, 8192], BF16, estR)
    ebf1 = alias("ebf1", [128, 8192], BF16, estR, 16384)
    WARE = sbt("WARE", [128, 16384], BF16)
    W0 = alias("W0", [128, 8192], BF16, WARE)
    W1 = alias("W1", [128, 8192], BF16, WARE, 16384)
    sbb = sbt("sbb", [128, 2048], BF16)
    Csb = alias("Csb", [128, 2048], BF16, sbb)
    slbt = sbt("slbt", [128, 1024], BF16)
    tmp0 = sbt("tmp0", [128, NT], F32)
    tmp1 = sbt("tmp1", [128, NT], F32)
    rsgb = alias("rsgb", [128, NT], BF16, tmp1)
    rowb = sbt("rowb", [1, 2048], BF16)
    qrecf = alias("qrecf", [4, 512], F32, tmp0)
    qrecb = alias("qrecb", [4, 1024], BF16, rowb)
    invSpp = sbt("invSpp", [128, 8], F32)
    pv = sbt("pv", [128, 90], F32)
    cb = sbt("cb", [128, 546], BF16)
    pbs = [es.enter_context(nc.psum_tensor(f"pb{i}", [128, 512], F32)) for i in range(8)]
    bst = [{"ap": p, "free": None} for p in pbs]

    def mkpool(ixs):
        pl = Pool.__new__(Pool)
        pl.items = [bst[i] for i in ixs]
        pl.i = 0
        return pl

    onescol = cb[:, 0:1]
    qind = lambda c: cb[:, 2 + 4 * c: 2 + 4 * (c + 1)]
    sel4 = lambda h: cb[0:4, 34 + 128 * h: 34 + 128 * (h + 1)]
    orow = cb[0:1, 34:162]
    ln1g = lambda c: pv[:, 0 + c: 1 + c]
    ln1b = lambda c: pv[:, 8 + c: 9 + c]
    ln2g = lambda c: pv[:, 16 + c: 17 + c]
    ln2b = lambda c: pv[:, 24 + c: 25 + c]
    bo_c = lambda c: pv[:, 32 + c: 33 + c]
    bff2_c = lambda c: pv[:, 40 + c: 41 + c]
    bd_c = lambda c: pv[:, 48 + c: 49 + c]
    bff1_c = lambda j: pv[:, 56 + j: 57 + j]
    epsap = pv[0:1, 88:89]
    onef = pv[0:1, 89:90]

    syQ = DmaQ(b, "sync", "syA", "syB")
    gpQ = DmaQ(b, "gpsimd", "gpA", "gpB")

    def load_x(dst, off, col0, width=NT, waits=()):
        return syQ.dma(lambda e: e.dma_start(
            ap3(dst, off, [[dst.shape[1], 128], [width, CH], [1, width]]),
            ap3(xkv, col0, [[N, 128], [128 * N, CH], [1, width]])), waits=waits)

    def load_w(dst, dram, cols, kc_cnt, row0=0, col0=0, width=1024, waits=(),
               q=None):
        return (q or gpQ).dma(lambda e: e.dma_start(
            ap3(dst, 0, [[dst.shape[1], 128], [width, kc_cnt], [1, width]]),
            ap3(dram, row0 * cols + col0, [[cols, 128], [128 * cols, kc_cnt], [1, width]])),
            waits=waits)

    ln_state = {"sbb_free0": 0, "sbb_free1": 0,
                "tmp_free0": ("se", 0), "tmp_free1": ("se", 0),
                "rowb_free0": 0, "rowb_free1": 0, "rowb_free2": 0, "rowb_free3": 0,
                "musq_free": 0}
    consts_edges = []

    def ln_batch(tiles, stat_pool, bcm_pool, bcr_pool):
        infos = []
        for ti in tiles:
            xw = list(ti.get("x_waits", ()))
            sbk = stat_pool.acquire()
            bank = sbk["ap"]
            xsrc_c = ti["xsrc_c"]
            for c in range(CH):
                xb = sbb[:, (c % 2) * 512:(c % 2) * 512 + 512]
                xq = sbb[:, 1024 + (c % 2) * 512:1024 + (c % 2) * 512 + 512]
                vw = xw + [("te", ln_state[f"sbb_free{c % 2}"])]
                ve1 = b.op("vector", lambda e, c=c, xb=xb, f=xsrc_c: e.tensor_copy(
                    xb, f(c)), waits=vw, inc="ve")
                ve2 = b.op("vector", lambda e, xb=xb, xq=xq: e.tensor_mul(xq, xb, xb),
                           inc="ve")
                b.op("tensor", lambda e, xb=xb, bank=bank, c=c: e.matmul(
                    bank[0:1, :], onescol, xb, start=(c == 0), stop=False,
                    skip_group_check=True),
                    waits=[("ve", ve1)] + (Pool.w(sbk) + consts_edges if c == 0 else []))
                te = b.op("tensor", lambda e, xq=xq, bank=bank, c=c: e.matmul(
                    bank[32:33, :], onescol, xq, start=(c == 0), stop=False,
                    skip_group_check=True), waits=[("ve", ve2)], inc="te")
                ln_state[f"sbb_free{c % 2}"] = te
            infos.append({"sbk": sbk, "stat_done": b.cnt["te"]})
        def rows(i):
            inf = infos[i]
            bank = inf["sbk"]["ap"]
            slot = i % 2
            mu_ap = rowb[0:1, slot * 1024: slot * 1024 + 512]
            rsig_ap = rowb[0:1, slot * 1024 + 512: slot * 1024 + 1024]
            se1 = b.op("scalar", lambda e, bank=bank, mu_ap=mu_ap: e.activation(
                mu_ap, bank[0:1, :], AF.Identity, scale=1.0 / 1024),
                waits=[("te", inf["stat_done"]), ("te", ln_state[f"rowb_free{slot}"])])
            b.op("vector", lambda e, mu_ap=mu_ap: e.tensor_mul(
                tmp0[0:1, :], mu_ap, mu_ap),
                waits=[("se", se1), ("se", ln_state["musq_free"]),
                       ln_state["tmp_free0"]])
            ve3 = b.op("vector", lambda e, bank=bank: e.scalar_tensor_tensor(
                bank[32:33, :], bank[32:33, :], 1.0 / 1024, tmp0[0:1, :],
                ALU.mult, ALU.subtract), inc="ve")
            b.op("scalar", lambda e, bank=bank: e.activation(
                bank[0:1, :], bank[32:33, :], AF.Ln, bias=epsap),
                waits=[("ve", ve3)])
            se2 = b.op("scalar", lambda e, bank=bank, rsig_ap=rsig_ap: e.activation(
                rsig_ap, bank[0:1, :], AF.Exp, scale=-0.5), inc="se")
            ln_state["musq_free"] = se2
            Pool.rel(inf["sbk"], "se", se2)
            inf.update(slot=slot, mu_ap=mu_ap, rsig_ap=rsig_ap, rows_done=se2)
        rows(0)
        for i, ti in enumerate(tiles):
            if i + 1 < len(tiles):
                rows(i + 1)
            inf = infos[i]
            bm = bcm_pool.acquire()
            b.op("tensor", lambda e, bank=bm["ap"], mu_ap=inf["mu_ap"]: e.matmul(
                bank[:, :], orow, mu_ap, start=True, stop=True,
                skip_group_check=True),
                waits=[("se", inf["rows_done"])] + Pool.w(bm))
            br = bcr_pool.acquire()
            b.op("tensor", lambda e, bank=br["ap"], rsig_ap=inf["rsig_ap"]: e.matmul(
                bank[:, :], orow, rsig_ap, start=True, stop=True,
                skip_group_check=True), waits=Pool.w(br), inc="te")
            bc_done = b.cnt["te"]
            ln_state[f"rowb_free{inf['slot']}"] = bc_done
            xsrc_c = ti["xsrc_c"]
            apply_out = ti["apply_out"]
            for c in range(CH):
                tmp = tmp0 if c % 2 == 0 else tmp1
                b.op("vector", lambda e, c=c, tmp=tmp, bank=bm["ap"], f=xsrc_c:
                     e.tensor_sub(tmp[:, :], f(c), bank[:, :]),
                     waits=[("te", bc_done), ln_state[f"tmp_free{c % 2}"]],
                     inc="ve")
                apply_out(c, tmp, br["ap"])
            Pool.rel(bm, "ve", b.cnt["ve"])
            Pool.rel(br, "ve", b.cnt["ve"])

    b.begin("PA")
    consts_edges.append(syQ.dma(lambda e: e.dma_start(pv[:, :], pvec_d.ap())))
    consts_edges.append(syQ.dma(lambda e: e.dma_start(cb[:, :], cbf_d.ap())))
    def load_half(dst, dram, h):
        return gpQ.dma(lambda e, dst=dst, dram=dram, h=h: e.dma_start(
            ap3(dst, h * 512, [[8192, 128], [1024, CH], [1, 512]]),
            ap3(dram, h * 512, [[D, 128], [128 * D, CH], [1, 512]])))
    gkh = [load_half(W1, wk_d, 0), None]
    gvh = [load_half(slotD1, wv_d, 0), None]
    gkh[1] = load_half(W1, wk_d, 1)
    gvh[1] = load_half(slotD1, wv_d, 1)
    gq = load_w(W0, wq_d, D, CH)

    lnb = pbs[7]
    st = {"bank_free": 0, "h1": [0] * 8, "xedge": [None] * 8,
          "xfree": [0, 0], "stat_done": [0] * 8, "rows": [None] * 8}
    st["xedge"][0] = load_x(xq0, 0, 0, waits=())
    st["xedge"][1] = load_x(xq1, 0, NT, waits=())
    xsl = [xq0, xq1]

    def lnA(t):
        xs = xsl[t % 2]
        for c in range(CH):
            xb = sbb[:, (c % 2) * 512:(c % 2) * 512 + 512]
            xq_ = sbb[:, 1024 + (c % 2) * 512:1024 + (c % 2) * 512 + 512]
            vw = [st["xedge"][t], ("te", ln_state[f"sbb_free{c % 2}"])]
            ve1 = b.op("vector", lambda e, c=c, xb=xb, xs=xs: e.tensor_copy(
                xb, xs[:, c * NT:(c + 1) * NT]), waits=vw, inc="ve")
            ve2 = b.op("scalar", lambda e, xb=xb, xq_=xq_: e.activation(
                xq_, xb, AF.Square), waits=[("ve", ve1)], inc="se")
            b.op("tensor", lambda e, xb=xb, c=c: e.matmul(
                lnb[0:1, :], onescol, xb, start=(c == 0), stop=False,
                skip_group_check=True),
                waits=[("ve", ve1)] + ([("ve", st["bank_free"])] + consts_edges
                                       if c == 0 else []))
            te = b.op("tensor", lambda e, xq_=xq_, c=c: e.matmul(
                lnb[32:33, :], onescol, xq_, start=(c == 0), stop=False,
                skip_group_check=True), waits=[("se", ve2)], inc="te")
            ln_state[f"sbb_free{c % 2}"] = te
        st["stat_done"][t] = b.cnt["te"]

    def lnB1(t):
        slot = t % 2
        mu_ap = rowb[0:1, slot * 1024: slot * 1024 + 512]
        rsig_ap = rowb[0:1, slot * 1024 + 512: slot * 1024 + 1024]
        st["mu_ap"], st["rsig_ap"] = mu_ap, rsig_ap
        se1 = b.op("scalar", lambda e, mu_ap=mu_ap: e.activation(
            mu_ap, lnb[0:1, :], AF.Identity, scale=1.0 / 1024),
            waits=[("te", st["stat_done"][t]),
                   ("te", ln_state[f"rowb_free{slot}"])])
        b.op("vector", lambda e, mu_ap=mu_ap: e.tensor_mul(
            tmp0[0:1, :], mu_ap, mu_ap),
            waits=[("se", se1), ("se", ln_state["musq_free"])])
        ve3 = b.op("vector", lambda e: e.scalar_tensor_tensor(
            lnb[32:33, :], lnb[32:33, :], 1.0 / 1024, tmp0[0:1, :],
            ALU.mult, ALU.subtract), inc="ve")
        st["tmprow"] = ve3
        st["ve3"] = ve3
        b.op("scalar", lambda e: e.activation(
            lnb[0:1, :], lnb[32:33, :], AF.Ln, bias=epsap), waits=[("ve", ve3)])
        se2 = b.op("scalar", lambda e, rsig_ap=rsig_ap: e.activation(
            rsig_ap, lnb[0:1, :], AF.Exp, scale=-0.5), inc="se")
        ln_state["musq_free"] = se2
        st["se2"] = se2

    def lnB2(t):
        xs = xsl[t % 2]
        slot = t % 2
        b.op("tensor", lambda e, mu_ap=st["mu_ap"]: e.matmul(
            lnb[:, :], orow, mu_ap, start=True, stop=True,
            skip_group_check=True),
            waits=[("se", st["se2"]), ("ve", st["ve3"])], inc="te")
        mubc = b.cnt["te"]
        ln_state[f"rowb_free{slot}"] = mubc
        for c in range(CH):
            b.op("vector", lambda e, c=c, xs=xs: e.tensor_sub(
                xs[:, c * NT:(c + 1) * NT], xs[:, c * NT:(c + 1) * NT],
                lnb[:, :]), waits=[("te", mubc)], inc="ve")
        st["p1ve"] = b.cnt["ve"]

    def lnB3(t):
        xs = xsl[t % 2]
        slot = t % 2
        b.op("tensor", lambda e, rsig_ap=st["rsig_ap"]: e.matmul(
            lnb[:, :], orow, rsig_ap, start=True, stop=True,
            skip_group_check=True), waits=[("ve", st["p1ve"])], inc="te")
        rsbc = b.cnt["te"]
        ln_state[f"rowb_free{slot}"] = rsbc
        for c in range(CH):
            b.op("vector", lambda e, c=c, t=t, xs=xs: e.scalar_tensor_tensor(
                h1[:, c * N + t * NT: c * N + (t + 1) * NT],
                xs[:, c * NT:(c + 1) * NT], ln1g(c), lnb[:, :],
                ALU.mult, ALU.mult), waits=[("te", rsbc)], inc="ve")
            b.op("vector", lambda e, c=c, t=t: e.tensor_scalar_add(
                h1[:, c * N + t * NT: c * N + (t + 1) * NT],
                h1[:, c * N + t * NT: c * N + (t + 1) * NT],
                ln1b(c)), inc="ve")
        st["bank_free"] = b.cnt["ve"]
        st["xfree"][slot] = b.cnt["ve"]
        st["h1"][t] = b.cnt["ve"]
        if t + 2 < 8:
            st["xedge"][t + 2] = load_x(xsl[t % 2], 0, (t + 2) * NT,
                                        waits=[("ve", st["xfree"][slot])])

    rot = mkpool([5, 6])
    ev = Pool([ebf0, ebf1])

    def kv_ssub(t, s_, eslab, first_tile_s):
        n0 = t * NT + s_ * 128
        res = {}
        for half in range(2):
            bk = rot.acquire()
            for kc in range(CH):
                w = (Pool.w(bk) + [gkh[half], ("ve", st["h1"][t])]) \
                    if kc == 0 else []
                b.op("tensor", lambda e, kc=kc, half=half, n0=n0, bk=bk: e.matmul(
                    bk["ap"][:, :],
                    h1[:, kc * N + n0: kc * N + n0 + 128],
                    W1[:, kc * 1024 + half * 512: kc * 1024 + (half + 1) * 512],
                    start=(kc == 0), stop=(kc == 7)), waits=w,
                    inc="te" if kc == 7 else None)
            tek = b.cnt["te"]
            se = b.op("scalar", lambda e, s_=s_, half=half, eslab=eslab, bk=bk:
                      e.activation(
                eslab[:, s_ * 1024 + half * 512: s_ * 1024 + (half + 1) * 512],
                bk["ap"][:, :], AF.Exp, scale=0.25), waits=[("te", tek)], inc="se")
            Pool.rel(bk, "se", se)
            res[f"e{half}"] = se
            bv = rot.acquire()
            for kc in range(CH):
                w = (Pool.w(bv) + [gvh[half]]) if kc == 0 else []
                b.op("tensor", lambda e, kc=kc, half=half, n0=n0, bv=bv: e.matmul(
                    bv["ap"][:, :],
                    h1[:, kc * N + n0: kc * N + n0 + 128],
                    slotD1[:, kc * 1024 + half * 512: kc * 1024 + (half + 1) * 512],
                    start=(kc == 0), stop=(kc == 7)), waits=w,
                    inc="te" if kc == 7 else None)
            tev = b.cnt["te"]
            ve = b.op("scalar", lambda e, s_=s_, half=half, eslab=eslab, bv=bv:
                      e.activation(
                eslab[:, 4096 + s_ * 1024 + half * 512:
                      4096 + s_ * 1024 + (half + 1) * 512],
                bv["ap"][:, :], AF.Identity), waits=[("te", tev)], inc="se")
            Pool.rel(bv, "se", ve)
            res[f"v{half}"] = ve
        first = first_tile_s
        for c in range(CH):
            b.op("tensor", lambda e, s_=s_, c=c, eslab=eslab, first=first: e.matmul(
                pbs[4][:, c:c + 1],
                eslab[:, s_ * 1024 + c * 128: s_ * 1024 + (c + 1) * 128],
                onescol,
                start=first, stop=False, skip_group_check=True),
                waits=[("se", res[f"e{c // 4}"])] + (consts_edges if first else []))
        for h in range(H):
            for ec in range(2):
                b.op("tensor", lambda e, s_=s_, h=h, ec=ec, eslab=eslab,
                     first=first: e.matmul(
                    pbs[h][:, ec * 256: (ec + 1) * 256],
                    eslab[:, 4096 + s_ * 1024 + h * 256 + ec * 128:
                          4096 + s_ * 1024 + h * 256 + (ec + 1) * 128],
                    eslab[:, s_ * 1024 + h * 256: s_ * 1024 + (h + 1) * 256],
                    start=first, stop=False, skip_group_check=True),
                    waits=[("se", res["v1"])],
                    inc="te" if (h == H - 1 and ec == 1) else None)

    lnA(0)
    lnB1(0)
    lnB2(0)
    lnB3(0)
    lnA(1)
    lnB1(1)
    lnB2(1)
    lnB3(1)
    for t in range(8):
        slot = ev.acquire()
        eslab = slot["ap"]
        kv_ssub(t, 0, eslab, t == 0)
        if t + 2 < 8:
            lnA(t + 2)
        kv_ssub(t, 1, eslab, False)
        if t + 2 < 8:
            lnB1(t + 2)
        kv_ssub(t, 2, eslab, False)
        if t + 2 < 8:
            lnB2(t + 2)
        kv_ssub(t, 3, eslab, False)
        if t + 2 < 8:
            lnB3(t + 2)
        Pool.rel(slot, "te", b.cnt["te"])
    kv_te = b.cnt["te"]
    p0_rows_done = max(ln_state["rowb_free0"], ln_state["rowb_free1"])
    p0_tmp_done = st["tmprow"]
    p0_sbb_done = max(ln_state["sbb_free0"], ln_state["sbb_free1"])

    qb_free = [0, 0]
    pb7f = [("ve", st["bank_free"])]
    pend = [None]

    def q_chain_a(P):
        par = P["par"]
        b.op("scalar", lambda e: e.activation(
            qrecf[0:4, :], pbs[7][0:4, :], AF.Ln),
            waits=[("te", P["teqs"]), ("ve", p0_tmp_done)])
        se = b.op("scalar", lambda e, par=par: e.activation(
            qrecb[0:4, par * 512:(par + 1) * 512], qrecf[0:4, :], AF.Exp,
            scale=-1.0), waits=[("te", p0_rows_done), ("te", qb_free[par])],
            inc="se")
        pb7f[0] = ("se", se)
        P["se"] = se

    def q_chain_b(P):
        t, par, se = P["t"], P["par"], P["se"]
        for hp in range(2):
            bA = rot.acquire()
            b.op("tensor", lambda e, hp=hp, par=par, bA=bA: e.matmul(
                bA["ap"][:, :], sel4(2 * hp),
                qrecb[0:4, par * 512:(par + 1) * 512],
                start=True, stop=True, skip_group_check=True),
                waits=[("se", se)] + Pool.w(bA))
            bB = rot.acquire()
            b.op("tensor", lambda e, hp=hp, par=par, bB=bB: e.matmul(
                bB["ap"][:, :], sel4(2 * hp + 1),
                qrecb[0:4, par * 512:(par + 1) * 512],
                start=True, stop=True, skip_group_check=True), waits=Pool.w(bB),
                inc="te")
            tqb = b.cnt["te"]
            for cc in range(4):
                c = hp * 4 + cc
                bank = bA if cc < 2 else bB
                b.op("vector", lambda e, c=c, t=t, bank=bank: e.tensor_mul(
                    Q[:, c * NOWN + t * NT: c * NOWN + (t + 1) * NT],
                    Q[:, c * NOWN + t * NT: c * NOWN + (t + 1) * NT],
                    bank["ap"][:, :]), waits=[("te", tqb)], inc="ve")
            Pool.rel(bA, "ve", b.cnt["ve"])
            Pool.rel(bB, "ve", b.cnt["ve"])
        qb_free[par] = b.cnt["te"]

    for t in range(4):
        q_ready = []
        for c in range(CH):
            if c == 1 and pend[0] is not None:
                q_chain_a(pend[0])
            elif c == 3 and pend[0] is not None:
                q_chain_b(pend[0])
                pend[0] = None
            bq = rot.acquire()
            for kc in range(CH):
                w = (Pool.w(bq) + [gq]) if kc == 0 else []
                b.op("tensor", lambda e, kc=kc, c=c, t=t, bq=bq: e.matmul(
                    bq["ap"][:, :],
                    W0[:, kc * 1024 + c * 128: kc * 1024 + (c + 1) * 128],
                    h1[:, kc * N + t * NT: kc * N + (t + 1) * NT],
                    start=(kc == 0), stop=(kc == 7)), waits=w,
                    inc="te" if kc == 7 else None)
            teq = b.cnt["te"]
            se = b.op("scalar", lambda e, c=c, t=t, bq=bq: e.activation(
                Q[:, c * NOWN + t * NT: c * NOWN + (t + 1) * NT],
                bq["ap"][:, :], AF.Exp, scale=0.25),
                waits=[("te", teq), ("ve", st["xfree"][c // 4])], inc="se")
            Pool.rel(bq, "se", se)
            q_ready.append(se)
        for c in range(CH):
            b.op("tensor", lambda e, c=c, t=t: e.matmul(
                pbs[7][0:4, :], qind(c),
                Q[:, c * NOWN + t * NT: c * NOWN + (t + 1) * NT],
                start=(c == 0), stop=(c == 7), skip_group_check=True),
                waits=[("se", q_ready[c])] + ([pb7f[0]] if c == 0 else []),
                inc="te" if c == 7 else None)
        pend[0] = {"t": t, "par": t % 2, "teqs": b.cnt["te"]}

    woslot = ebf0
    go = load_w(woslot, wo_d, D, CH, waits=Pool.w(ev.items[0]))
    b.op("scalar", lambda e: e.activation(tmp0[:, 0:8], pbs[4][:, 0:8], AF.Ln),
         waits=[("te", kv_te), ("ve", p0_tmp_done)])
    sei = b.op("scalar", lambda e: e.activation(invSpp[:, :], tmp0[:, 0:8],
                                                AF.Exp, scale=-1.0), inc="se")
    cev = []
    for h in range(H):
        ve = b.op("scalar", lambda e, h=h: e.activation(
            Csb[:, h * 512:(h + 1) * 512], pbs[h][:, :], AF.Identity),
            waits=[("te", kv_te), ("te", p0_sbb_done)], inc="se")
        cev.append(ve)
    q_chain_a(pend[0])
    rot2 = mkpool([0, 1])
    for h in range(H):
        for dc in range(2):
            if h == 0 and dc == 1 and pend[0] is not None:
                q_chain_b(pend[0])
                pend[0] = None
            gc = 2 * h + dc
            for jh in range(2):
                bk = rot2.acquire()
                for ec in range(2):
                    w = (Pool.w(bk) + [("se", cev[H - 1]), go]) if ec == 0 else []
                    b.op("tensor", lambda e, h=h, dc=dc, jh=jh, ec=ec, bk=bk: e.matmul(
                        bk["ap"][:, :],
                        Csb[:, h * 512 + ec * 256 + dc * 128:
                            h * 512 + ec * 256 + (dc + 1) * 128],
                        woslot[:, (2 * h + ec) * 1024 + jh * 512:
                               (2 * h + ec) * 1024 + (jh + 1) * 512],
                        start=(ec == 0), stop=(ec == 1)), waits=w,
                        inc="te" if ec == 1 else None)
                tcw = b.cnt["te"]
                secw = b.op("scalar", lambda e, gc=gc, jh=jh, bk=bk: e.activation(
                    CWt[:, gc * 1024 + jh * 512: gc * 1024 + (jh + 1) * 512],
                    bk["ap"][:, :], AF.Identity, scale=invSpp[:, gc:gc + 1]),
                    waits=[("te", tcw), ("se", sei)], inc="se")
                Pool.rel(bk, "se", secw)
    qnorm_all = b.cnt["ve"]
    b.end()

    def tap(name, src_handle, shape, dtype):
        th = nc.dram_tensor(name, shape, dtype, kind="ExternalOutput")
        b.begin("tap")
        ov = syQ.dma(lambda e: e.dma_start(th.ap(), src_handle[:, :]))
        b.op("sync", lambda e, ov=ov: e.wait_ge(b.sem[ov[0]], ov[1]))
        b.end()

    if debug:
        tap("tap_h1", h1, [128, CH * N], BF16)
        dbg["h1"] = "tap_h1"
        tap("tap_q", Q, [128, CH * NOWN], BF16)
        dbg["Qn"] = "tap_q"
        tap("tap_cw", CWt, [128, CH * D], BF16)
        dbg["CW"] = "tap_cw"
    if phases <= 3:
        return (b, dbg) if debug else b

    b.begin("PB")

    def mk_ln(t, xsrc_c, kind, extra_stat_waits=(), cast_eng="scalar"):
        odd = (t % 2 == 1) and (kind in ("ln2", "ln4") or (kind == "ln3" and t == 1))
        lnS = bst[4] if odd else bst[6]
        lnB_ = bst[5] if odd else bst[7]
        ch = {"lnS": lnS, "lnB": lnB_}

        cast_sem = "gp2" if cast_eng == "gpsimd" else "se"

        def stats_c(c, xw=()):
            xb = sbb[:, (c % 2) * 512:(c % 2) * 512 + 512]
            xq_ = sbb[:, 1024 + (c % 2) * 512:1024 + (c % 2) * 512 + 512]
            vw = (list(extra_stat_waits) + list(xw)
                  + [("te", ln_state[f"sbb_free{c % 2}"])])
            if cast_eng == "p5":
                ce = ("gpsimd" if c % 2 == 0
                      else ("scalar" if c in (1, 5) else "vector"))
            else:
                ce = cast_eng
            if ce == "gpsimd":
                e1 = ("gp2", b.op("gpsimd", lambda e, c=c, xb=xb: e.tensor_copy(
                    xb, xsrc_c(c)), waits=vw, inc="gp2"))
            elif ce == "vector":
                e1 = ("ve", b.op("vector", lambda e, c=c, xb=xb: e.tensor_copy(
                    xb, xsrc_c(c)), waits=vw, inc="ve"))
            else:
                e1 = ("se", b.op("scalar", lambda e, c=c, xb=xb: e.activation(
                    xb, xsrc_c(c), AF.Identity), waits=vw, inc="se"))
            if cast_eng == "p5":
                e2 = ("ve", b.op("vector", lambda e, xb=xb, xq_=xq_: e.tensor_mul(
                    xq_, xb, xb), waits=[e1], inc="ve"))
            elif ce == "gpsimd":
                e2 = ("gp2", b.op("gpsimd", lambda e, xb=xb, xq_=xq_: e.tensor_mul(
                    xq_, xb, xb), inc="gp2"))
            else:
                e2 = ("se", b.op("scalar", lambda e, xb=xb, xq_=xq_: e.activation(
                    xq_, xb, AF.Square), inc="se"))
            b.op("tensor", lambda e, xb=xb, c=c: e.matmul(
                lnS["ap"][0:1, :], onescol, xb, start=(c == 0), stop=False,
                skip_group_check=True),
                waits=[e1] + (Pool.w(lnS) if c == 0 else []))
            te = b.op("tensor", lambda e, xq_=xq_, c=c: e.matmul(
                lnS["ap"][32:33, :], onescol, xq_, start=(c == 0), stop=False,
                skip_group_check=True), waits=[e2], inc="te")
            ln_state[f"sbb_free{c % 2}"] = te
            if c == CH - 1:
                ch["stat_done"] = b.cnt["te"]

        def stats():
            for c in range(CH):
                stats_c(c)

        def rows():
            slot = t % 2
            ch["slot"] = slot
            mu_ap = rowb[0:1, slot * 1024: slot * 1024 + 512]
            rsig_ap = rowb[0:1, slot * 1024 + 512: slot * 1024 + 1024]
            ch["mu_ap"], ch["rsig_ap"] = mu_ap, rsig_ap
            se1 = b.op("scalar", lambda e, mu_ap=mu_ap: e.activation(
                mu_ap, lnS["ap"][0:1, :], AF.Identity, scale=1.0 / 1024),
                waits=[("te", ch["stat_done"]),
                       ("te", ln_state[f"rowb_free{slot}"])])
            ch["se1"] = se1
            b.op("vector", lambda e, mu_ap=mu_ap: e.tensor_mul(
                tmp0[0:1, :], mu_ap, mu_ap),
                waits=[("se", se1), ("se", ln_state["musq_free"]),
                       ln_state["tmp_free0"]])
            ve3 = b.op("vector", lambda e: e.scalar_tensor_tensor(
                lnS["ap"][32:33, :], lnS["ap"][32:33, :], 1.0 / 1024, tmp0[0:1, :],
                ALU.mult, ALU.subtract), inc="ve")
            ch["ve3"] = ve3
            b.op("scalar", lambda e: e.activation(
                lnS["ap"][0:1, :], lnS["ap"][32:33, :], AF.Ln, bias=epsap),
                waits=[("ve", ve3)])
            se2 = b.op("scalar", lambda e, rsig_ap=rsig_ap: e.activation(
                rsig_ap, lnS["ap"][0:1, :], AF.Exp, scale=-0.5), inc="se")
            ln_state["musq_free"] = se2
            ch["se2"] = se2

        def bcs():
            b.op("tensor", lambda e: e.matmul(
                lnB_["ap"][:, :], orow, ch["mu_ap"], start=True, stop=True,
                skip_group_check=True),
                waits=[("se", ch["se1"])] + Pool.w(lnB_), inc="te")
            te = b.op("tensor", lambda e: e.matmul(
                lnS["ap"][:, :], orow, ch["rsig_ap"], start=True, stop=True,
                skip_group_check=True),
                waits=[("se", ch["se2"]), ("ve", ch["ve3"])], inc="te")
            ln_state[f"rowb_free{ch['slot']}"] = te
            ch["bc_done"] = te

        ch["stats"] = stats
        ch["stats_c"] = stats_c
        ch["rows"] = rows
        ch["bcs"] = bcs
        return ch

    g_loads = {0: (load_w(W0, wff1_d, FF, CH, col0=0),
                   load_w(W1, wff2_d, D, CH, row0=0))}
    rot3 = mkpool([0, 1, 2])

    sy_chunk = {}

    def load_x_chunk(t, m, waits=()):
        toff = (t % 2) * 4096
        sy_chunk[(t, m)] = syQ.dma(lambda e, t=t, m=m, toff=toff: e.dma_start(
            ap3(estR, toff + m * NT, [[8192, 128], [1, NT]]),
            ap3(xkv, t * NT + m * 128 * N, [[N, 128], [1, NT]])),
            waits=list(waits))

    x1_ve = [0] * 4
    L2 = []
    h2_ready = [[] for _ in range(4)]
    attn_tile_te = [0] * 4

    def l2_apply(t, clo=0, chi=CH):
        ch = L2[t]
        lnS, lnB_ = ch["lnS"], ch["lnB"]
        for c in range(clo, chi):
            tmp = tmp0 if c % 2 == 0 else tmp1
            b.op("vector", lambda e, c=c, t=t, tmp=tmp: e.tensor_sub(
                tmp[:, :], R1[:, c * NOWN + t * NT: c * NOWN + (t + 1) * NT],
                lnB_["ap"][:, :]),
                waits=[("te", ch["bc_done"]),
                       ln_state[f"tmp_free{c % 2}"]], inc="ve")
            ve = b.op("vector", lambda e, tmp=tmp: e.tensor_mul(
                tmp[:, :], tmp[:, :], lnS["ap"][:, :]), inc="ve")
            if t >= 2 and c % 2 == 1:
                gp = b.op("gpsimd", lambda e, c=c, t=t, tmp=tmp: e.tensor_scalar(
                    h2x[:, c * NOWN + t * NT: c * NOWN + (t + 1) * NT],
                    tmp[:, :], ln2g(c), ln2b(c), ALU.mult, ALU.add),
                    waits=[("ve", ve), ("te", attn_tile_te[t])], inc="gp2")
                ln_state[f"tmp_free{c % 2}"] = ("gp2", gp)
            else:
                se = b.op("scalar", lambda e, c=c, t=t, tmp=tmp: e.activation(
                    h2x[:, c * NOWN + t * NT: c * NOWN + (t + 1) * NT], tmp[:, :],
                    AF.Identity, scale=ln2g(c), bias=ln2b(c)),
                    waits=[("ve", ve), ("te", attn_tile_te[t])], inc="se")
                ln_state[f"tmp_free{c % 2}"] = ("se", se)
        if chi == CH:
            Pool.rel(lnB_, "ve", b.cnt["ve"])
            Pool.rel(lnS, "ve", b.cnt["ve"])
            h2_ready[t] = [("se", b.cnt["se"]), ("gp2", b.cnt["gp2"])]

    def attn_hook(t, m):
        if t == 1 and m == 0:
            L2[0]["stats"]()
        elif t == 1 and m == 4:
            L2[0]["rows"]()
        elif t == 2 and m == 0:
            L2[0]["bcs"]()
        elif t == 2 and m == 4:
            l2_apply(0, 0, 3)
        elif t == 2 and m == 6:
            L2[1]["stats"]()
        elif t == 3 and m == 0:
            l2_apply(0, 3, CH)
        elif t == 3 and m == 2:
            L2[1]["rows"]()
        elif t == 3 and m == 5:
            L2[1]["bcs"]()

    for m in range(CH):
        load_x_chunk(0, m)
    for m in range(CH):
        load_x_chunk(1, m)
    for t in range(4):
        toff = (t % 2) * 4096
        th = estR
        for m in range(CH):
            attn_hook(t, m)
            bk = rot3.acquire()
            for kc in range(CH):
                w = (Pool.w(bk)
                     + ([("ve", qnorm_all)] if t == 0 and m == 0 else [])
                     ) if kc == 0 else []
                b.op("tensor", lambda e, m=m, kc=kc, t=t, bk=bk: e.matmul(
                    bk["ap"][:, :],
                    CWt[:, kc * 1024 + m * 128: kc * 1024 + (m + 1) * 128],
                    Q[:, kc * NOWN + t * NT: kc * NOWN + (t + 1) * NT],
                    start=(kc == 0), stop=(kc == 7)), waits=w,
                    inc="te" if kc == 7 else None)
            attn_te = b.cnt["te"]
            ve = b.op("vector", lambda e, m=m, t=t, th=th, toff=toff, bk=bk:
                      e.scalar_tensor_tensor(
                R1[:, m * NOWN + t * NT: m * NOWN + (t + 1) * NT],
                bk["ap"][:, :], bo_c(m),
                th[:, toff + m * NT: toff + (m + 1) * NT],
                ALU.add, ALU.add),
                waits=[sy_chunk[(t, m)], ("te", attn_te)], inc="ve")
            Pool.rel(bk, "ve", ve)
            if t < 2:
                load_x_chunk(t + 2, m, waits=[("ve", ve)])
        x1_ve[t] = b.cnt["ve"]
        attn_tile_te[t] = b.cnt["te"]
        xsrc = lambda c, t=t: R1[:, c * NOWN + t * NT: c * NOWN + (t + 1) * NT]
        L2.append(mk_ln(t, xsrc, "ln2", extra_stat_waits=[("ve", x1_ve[t])],
                        cast_eng="p5" if t >= 2 else "scalar"))
    attn_all = b.cnt["te"]
    x0_read_ve = b.cnt["ve"]

    wslots = [W0, W1, slotD0, slotD1]
    slot_free = [0, 0, 0, 0]
    g_loads[1] = (load_w(wslots[2], wff1_d, FF, CH, col0=1024,
                         waits=[("te", attn_all)]),
                  load_w(wslots[3], wff2_d, D, CH, row0=1024,
                         waits=[("te", attn_all)]))
    rotA = mkpool([0, 1, 2])
    rotB = mkpool([3, 4, 5])
    g_read_te = [0, 0, 0, 0]
    r1_done = [0, 0, 0, 0]
    shv01 = [0, 0]
    q3b_m_te = [0] * CH
    q2_done = [0, 0]
    q3a_done = [0]
    q3b_done = [0]
    for qq in range(4):
        s1 = wslots[(2 * qq) % 4]
        s2 = wslots[(2 * qq + 1) % 4]
        g1, g2 = g_loads[qq]
        gelu_se = [0, 0, 0, 0]
        for t in range(4):
            for jj in range(CH):
                if qq == 0:
                    if t == 0 and jj == 0:
                        l2_apply(1)
                    elif t == 0 and jj == 2:
                        L2[2]["stats"]()
                    elif t == 0 and jj == 5:
                        L2[2]["rows"]()
                    elif t == 0 and jj == 7:
                        L2[2]["bcs"]()
                    elif t == 1 and jj == 0:
                        l2_apply(2)
                    elif t == 1 and jj == 2:
                        L2[3]["stats"]()
                    elif t == 1 and jj == 5:
                        L2[3]["rows"]()
                    elif t == 1 and jj == 7:
                        L2[3]["bcs"]()
                    elif t == 2 and jj == 0:
                        l2_apply(3)
                bk = rotA.acquire()
                for kc in range(CH):
                    w = (Pool.w(bk) + [g1, ("te", g_read_te[t]),
                                       *h2_ready[t],
                                       ("ve", x0_read_ve)]
                         + ([sy_chunk[(3, 6)], sy_chunk[(3, 7)]]
                            if qq == 0 and t == 0 and jj == 0 else [])
                         ) if kc == 0 else []
                    b.op("tensor", lambda e, jj=jj, kc=kc, t=t, s1=s1, bk=bk: e.matmul(
                        bk["ap"][:, :],
                        s1[:, kc * 1024 + jj * 128: kc * 1024 + (jj + 1) * 128],
                        h2x[:, kc * NOWN + t * NT: kc * NOWN + (t + 1) * NT],
                        start=(kc == 0), stop=(kc == 7)), waits=w,
                        inc="te" if kc == 7 else None)
                tef = b.cnt["te"]
                se = b.op("scalar", lambda e, jj=jj, t=t, qq=qq, bk=bk: e.activation(
                    Gq[:, jj * NOWN + t * NT: jj * NOWN + (t + 1) * NT],
                    bk["ap"][:, :], AF.Gelu_apprx_tanh, bias=bff1_c(qq * 8 + jj)),
                    waits=[("te", tef)], inc="se")
                Pool.rel(bk, "se", se)
            gelu_se[t] = b.cnt["se"]
        slot_free[(2 * qq) % 4] = b.cnt["te"]
        if qq == 3:
            q3a_done[0] = b.cnt["te"]
        nq = qq + 2
        if nq < 4:
            g1n = load_w(wslots[(2 * nq) % 4], wff1_d, FF, CH, col0=nq * 1024,
                         waits=[("te", slot_free[(2 * nq) % 4])])
        for t in range(4):
            for m in range(CH):
                bk = rotB.acquire()
                for jc in range(CH):
                    w = (Pool.w(bk) + [g2, ("se", gelu_se[t])]) \
                        if jc == 0 else []
                    b.op("tensor", lambda e, m=m, jc=jc, t=t, s2=s2, bk=bk: e.matmul(
                        bk["ap"][:, :],
                        s2[:, jc * 1024 + m * 128: jc * 1024 + (m + 1) * 128],
                        Gq[:, jc * NOWN + t * NT: jc * NOWN + (t + 1) * NT],
                        start=(jc == 0), stop=(jc == 7)), waits=w,
                        inc="te" if jc == 7 else None)
                teb = b.cnt["te"]
                if qq == 0:
                    ve = b.op("vector", lambda e, m=m, t=t, bk=bk:
                              e.scalar_tensor_tensor(
                        R1[:, m * NOWN + t * NT: m * NOWN + (t + 1) * NT],
                        R1[:, m * NOWN + t * NT: m * NOWN + (t + 1) * NT],
                        bff2_c(m), bk["ap"][:, :],
                        ALU.add, ALU.add), waits=[("te", teb)], inc="ve")
                else:
                    ve = b.op("vector", lambda e, m=m, t=t, bk=bk: e.tensor_add(
                        R1[:, m * NOWN + t * NT: m * NOWN + (t + 1) * NT],
                        R1[:, m * NOWN + t * NT: m * NOWN + (t + 1) * NT],
                        bk["ap"][:, :]), waits=[("te", teb)], inc="ve")
                Pool.rel(bk, "ve", ve)
                if qq == 3:
                    r1_done[t] = b.cnt["ve"]
                    if t < 2:
                        sh_ = shad if t % 2 == 0 else shad2
                        b.op("scalar", lambda e, m=m, t=t, sh_=sh_: e.activation(
                            sh_[:, m * NT:(m + 1) * NT],
                            R1[:, m * NOWN + t * NT: m * NOWN + (t + 1) * NT],
                            AF.Identity),
                            waits=[("ve", ve), ("te", q3a_done[0])], inc="se")
                        if m == CH - 1:
                            shv01[t] = b.cnt["se"]
                    if t == 3:
                        q3b_m_te[m] = b.cnt["te"]
            g_read_te[t] = b.cnt["te"]
        slot_free[(2 * qq + 1) % 4] = b.cnt["te"]
        if qq == 2:
            q2_done[0] = b.cnt["te"]
            gg0 = load_w(W0, wgT_d, HID, CH, col0=0,
                         waits=[("te", slot_free[0])], q=syQ)
            gu0 = load_w(W1, wuT_d, HID, CH, col0=0,
                         waits=[("te", slot_free[1])], q=syQ)
        if qq == 3:
            q3b_done[0] = b.cnt["te"]
        if nq < 4:
            g_loads[nq] = (g1n, load_w(wslots[(2 * nq + 1) % 4], wff2_d, D, CH,
                                       row0=nq * 1024,
                                       waits=[("te", slot_free[(2 * nq + 1) % 4])]))
    gds = [gpQ.dma(lambda e, q=q: e.dma_start(
        ap3(slotD1, q * 256, [[8192, 128], [1024, CH], [1, 256]]),
        ap3(wdd_d, q * 256, [[D, 128], [128 * D, CH], [1, 256]])),
        waits=[("te", q3b_m_te[2 * q + 1])]) for q in range(4)]

    est = Pool([(estR, 0), (estR, 4096)])
    est.items[0]["free"] = ("te", q3b_done[0])
    est.items[1]["free"] = ("te", q3b_done[0])
    rot5 = mkpool([0, 1, 2])
    L3 = []
    x3s_ready = [0] * 4
    u_evac = [0] * 4

    def l3_apply(t):
        ch = L3[t]
        lnS, lnB_ = ch["lnS"], ch["lnB"]
        sc = b.op("scalar", lambda e: e.activation(
            rsgb[:, :], lnS["ap"][:, :], AF.Identity),
            waits=[("te", ch["bc_done"]), ("gp2", ln_state.get("rsgb_free", 0))],
            inc="se")
        for c in range(CH):
            ve = b.op("vector", lambda e, c=c, t=t: e.tensor_sub(
                R1[:, c * NOWN + t * NT: c * NOWN + (t + 1) * NT],
                R1[:, c * NOWN + t * NT: c * NOWN + (t + 1) * NT],
                lnB_["ap"][:, :]), waits=[("te", ch["bc_done"])], inc="ve")
            gp = b.op("gpsimd", lambda e, c=c, t=t: e.tensor_mul(
                R1[:, c * NOWN + t * NT: c * NOWN + (t + 1) * NT],
                R1[:, c * NOWN + t * NT: c * NOWN + (t + 1) * NT],
                rsgb[:, :]), waits=[("se", sc), ("ve", ve)], inc="gp2")
            b.op("scalar", lambda e, c=c, t=t, gp=gp: e.activation(
                h2x[:, c * NOWN + t * NT: c * NOWN + (t + 1) * NT],
                R1[:, c * NOWN + t * NT: c * NOWN + (t + 1) * NT],
                AF.Identity, scale=S127),
                waits=[("te", q3a_done[0]), ("gp2", gp)], inc="se")
        ln_state["rsgb_free"] = b.cnt["gp2"]
        Pool.rel(lnB_, "ve", b.cnt["ve"])
        Pool.rel(lnS, "se", sc)
        x3s_ready[t] = b.cnt["se"]

    shad_free = [0, 0]
    for t in range(4):
        sh = shad if t % 2 == 0 else shad2
        slot = est.acquire()
        th, toff = slot["ap"]
        syv = load_x(th, toff, t * NT, waits=Pool.w(slot))
        xsrc = lambda c, t=t: R1[:, c * NOWN + t * NT: c * NOWN + (t + 1) * NT]
        L3.append(mk_ln(t, xsrc, "ln3", cast_eng="p5"))
        if t >= 2:
            shv = 0
            for c in range(CH):
                shv = b.op("scalar", lambda e, c=c, t=t, sh=sh: e.activation(
                    sh[:, c * NT:(c + 1) * NT],
                    R1[:, c * NOWN + t * NT: c * NOWN + (t + 1) * NT],
                    AF.Identity),
                    waits=[("te", shad_free[t % 2]), ("ve", r1_done[t])],
                    inc="se")
        else:
            shv = shv01[t]
        if t >= 1:
            L3[t - 1]["rows"]()
        ve_hist = [0] * CH
        for m in range(CH):
            if m == 2 and t >= 1:
                L3[t - 1]["bcs"]()
            bk = rot5.acquire()
            gd = gds[m // 2]
            for kc in range(CH):
                w = (Pool.w(bk) + [gd, ("se", shv)]) if kc == 0 else []
                b.op("tensor", lambda e, m=m, kc=kc, bk=bk, sh=sh: e.matmul(
                    bk["ap"][:, :],
                    slotD1[:, kc * 1024 + m * 128: kc * 1024 + (m + 1) * 128],
                    sh[:, kc * NT:(kc + 1) * NT],
                    start=(kc == 0), stop=(kc == 7)), waits=w,
                    inc="te" if kc == 7 else None)
            ted = b.cnt["te"]
            if m == CH - 1:
                shad_free[t % 2] = ted
            ve = b.op("vector", lambda e, m=m, t=t, th=th, toff=toff, bk=bk:
                      e.scalar_tensor_tensor(
                R1[:, m * NOWN + t * NT: m * NOWN + (t + 1) * NT],
                bk["ap"][:, :], bd_c(m),
                th[:, toff + m * NT: toff + (m + 1) * NT],
                ALU.add, ALU.add), waits=[syv, ("te", ted)], inc="ve")
            Pool.rel(bk, "ve", ve)
            ve_hist[m] = ve
            if t < 3 and m >= 2:
                L3[t]["stats_c"](m - 2, xw=[("ve", ve_hist[m - 2])])
            if m == 4 and t >= 1:
                l3_apply(t - 1)
        u_evac[t] = b.cnt["ve"]
        Pool.rel(slot, "ve", u_evac[t])
        if t < 3:
            L3[t]["stats_c"](6, xw=[("ve", ve_hist[6])])
            L3[t]["stats_c"](7, xw=[("ve", ve_hist[7])])
        else:
            for c in range(CH):
                L3[3]["stats_c"](c, xw=[("ve", u_evac[3])])
    u_all = b.cnt["ve"]
    wd_all = b.cnt["te"]

    gwd0 = load_w(wslotE0, wdT_d, D, CH, row0=0, waits=[("ve", u_all)],
                  q=syQ)

    rotA = mkpool([0, 1, 2])
    rotB = mkpool([3, 4, 5])
    wd_free = [0, 0]
    f_read_te = [0, 0, 0, 0]
    silu_par = 0
    silu_free = 0
    loads6 = {0: (gg0, gu0, gwd0)}
    L4 = [None] * 4
    out_edges = []
    oslot_free = [("te", q3a_done[0]), ("te", q3a_done[0])]

    def l4_apply(t):
        ch = L4[t]
        lnS, lnB_ = ch["lnS"], ch["lnB"]
        xs = xsl[t % 2]
        sc = b.op("scalar", lambda e: e.activation(
            rsgb[:, :], lnS["ap"][:, :], AF.Identity),
            waits=[("te", ch["bc_done"]), ("gp2", ln_state.get("rsgb_free", 0))],
            inc="se")
        gp = 0
        for c in range(CH):
            ve = b.op("vector", lambda e, c=c, t=t: e.tensor_sub(
                R1[:, c * NOWN + t * NT: c * NOWN + (t + 1) * NT],
                R1[:, c * NOWN + t * NT: c * NOWN + (t + 1) * NT],
                lnB_["ap"][:, :]), waits=[("te", ch["bc_done"])], inc="ve")
            gp = b.op("gpsimd", lambda e, c=c, t=t, xs=xs: e.tensor_mul(
                xs[:, c * NT:(c + 1) * NT],
                R1[:, c * NOWN + t * NT: c * NOWN + (t + 1) * NT],
                rsgb[:, :]),
                waits=[("se", sc), ("ve", ve), oslot_free[t % 2],
                       ("te", r2a_done[0])], inc="gp2")
        ln_state["rsgb_free"] = b.cnt["gp2"]
        Pool.rel(lnB_, "ve", b.cnt["ve"])
        Pool.rel(lnS, "se", sc)
        ov = syQ.dma(lambda e, t=t, xs=xs: e.dma_start(
            ap3(yout, t * NT, [[NOWN, 128], [128 * NOWN, CH], [1, NT]]),
            ap3(xs, 0, [[xs.shape[1], 128], [NT, CH], [1, NT]])),
            waits=[("gp2", gp)])
        oslot_free[t % 2] = ov
        out_edges.append(ov)

    r2a_done = [0]
    down_evac = [0] * 4
    for r in range(3):
        gg, gu, gwd = loads6[r]
        wds = wslotE0 if r % 2 == 0 else wslotE1
        fve = [0, 0, 0, 0]
        for t in range(4):
            for hc in range(CH):
                if r == 0 and t == 0 and hc == 4:
                    L3[3]["rows"]()
                    L3[3]["bcs"]()
                if r == 0 and t == 1 and hc == 0:
                    l3_apply(3)
                bA = rotA.acquire()
                for kc in range(CH):
                    w = (Pool.w(bA) + [gg, ("te", f_read_te[t]),
                                       ("se", x3s_ready[t])]) \
                        if kc == 0 else []
                    b.op("tensor", lambda e, hc=hc, kc=kc, t=t, bA=bA: e.matmul(
                        bA["ap"][:, :],
                        W0[:, kc * 1024 + hc * 128: kc * 1024 + (hc + 1) * 128],
                        h2x[:, kc * NOWN + t * NT: kc * NOWN + (t + 1) * NT],
                        start=(kc == 0), stop=(kc == 7)), waits=w,
                        inc="te" if kc == 7 else None)
                teg = b.cnt["te"]
                bB = rotB.acquire()
                for kc in range(CH):
                    w = (Pool.w(bB) + [gu]) if kc == 0 else []
                    b.op("tensor", lambda e, hc=hc, kc=kc, t=t, bB=bB: e.matmul(
                        bB["ap"][:, :],
                        W1[:, kc * 1024 + hc * 128: kc * 1024 + (hc + 1) * 128],
                        h2x[:, kc * NOWN + t * NT: kc * NOWN + (t + 1) * NT],
                        start=(kc == 0), stop=(kc == 7)), waits=w,
                        inc="te" if kc == 7 else None)
                teu = b.cnt["te"]
                sl = slbt[:, silu_par * 512: (silu_par + 1) * 512]
                silu_par ^= 1
                se = b.op("scalar", lambda e, sl=sl, bA=bA: e.activation(
                    sl, bA["ap"][:, :], AF.Silu),
                    waits=[("te", teg), ("ve", silu_free)], inc="se")
                Pool.rel(bA, "se", se)
                ve = b.op("vector", lambda e, hc=hc, t=t, sl=sl, bB=bB:
                          e.scalar_tensor_tensor(
                    Ftn[:, hc * NOWN + t * NT: hc * NOWN + (t + 1) * NT],
                    sl, S127, bB["ap"][:, :], ALU.mult, ALU.mult),
                    waits=[("se", se), ("te", teu), ("te", wd_all)], inc="ve")
                Pool.rel(bB, "ve", ve)
                silu_free = ve
            fve[t] = b.cnt["ve"]
        a_done = b.cnt["te"]
        if r == 2:
            r2a_done[0] = a_done
        if r + 1 < 3:
            ggn = load_w(W0, wgT_d, HID, CH, col0=(r + 1) * 1024, q=syQ,
                         waits=[("te", a_done)])
            gun = load_w(W1, wuT_d, HID, CH, col0=(r + 1) * 1024, q=syQ,
                         waits=[("te", a_done)])
            gwn = load_w(wslotE0 if (r + 1) % 2 == 0 else wslotE1, wdT_d, D, CH, q=syQ,
                         row0=(r + 1) * 1024,
                         waits=[("te", wd_free[(r + 1) % 2]), ("ve", u_all)])
            loads6[r + 1] = (ggn, gun, gwn)
        for t in range(4):
            if r == 2 and t >= 1:
                L4[t - 1]["rows"]()
            for m in range(CH):
                if r == 2 and t >= 1 and m == 2:
                    L4[t - 1]["bcs"]()
                if r == 2 and t >= 1 and m == 4:
                    l4_apply(t - 1)
                bk = rotA.acquire()
                for hc in range(CH):
                    w = (Pool.w(bk) + [gwd, ("ve", fve[t])]) if hc == 0 else []
                    b.op("tensor", lambda e, m=m, hc=hc, t=t, wds=wds, bk=bk: e.matmul(
                        bk["ap"][:, :],
                        wds[:, hc * 1024 + m * 128: hc * 1024 + (m + 1) * 128],
                        Ftn[:, hc * NOWN + t * NT: hc * NOWN + (t + 1) * NT],
                        start=(hc == 0), stop=(hc == 7)), waits=w,
                        inc="te" if hc == 7 else None)
                ted = b.cnt["te"]
                ve = b.op("vector", lambda e, m=m, t=t, bk=bk: e.tensor_add(
                    R1[:, m * NOWN + t * NT: m * NOWN + (t + 1) * NT],
                    R1[:, m * NOWN + t * NT: m * NOWN + (t + 1) * NT],
                    bk["ap"][:, :]), waits=[("te", ted)], inc="ve")
                Pool.rel(bk, "ve", ve)
            f_read_te[t] = b.cnt["te"]
            if r == 2:
                down_evac[t] = b.cnt["ve"]
                xsrc = lambda c, t=t: R1[:, c * NOWN + t * NT:
                                         c * NOWN + (t + 1) * NT]
                L4[t] = mk_ln(t, xsrc, "ln4",
                              extra_stat_waits=[("ve", down_evac[t])],
                              cast_eng="p5")
                L4[t]["stats"]()
        wd_free[r % 2] = b.cnt["te"]
    L4[3]["rows"]()
    L4[3]["bcs"]()
    l4_apply(3)
    kf = syQ.k
    b.op("sync", lambda e, v=16 * ((kf + 1) // 2): e.wait_ge(b.sem["syA"], v))
    b.op("sync", lambda e, v=16 * (kf // 2): e.wait_ge(b.sem["syB"], v))
    b.end()

    if debug:
        return b, dbg
    return b


def host_inputs(inputs):
    x = np.ascontiguousarray(np.asarray(inputs["L0_attn_x"], np.float32))
    assert x.shape == (4, N, D)

    def pp(v, ch):
        return np.ascontiguousarray(np.asarray(v, np.float32).reshape(ch, 128).T)

    pvec = np.zeros((128, 90), np.float32)
    pvec[:, 88] = EPS
    pvec[:, 89] = 1.0
    pvec[:, 0:8] = pp(inputs["L0_attn_ln1_g"], 8)
    pvec[:, 8:16] = pp(inputs["L0_attn_ln1_b"], 8)
    pvec[:, 16:24] = pp(inputs["L0_attn_ln2_g"], 8)
    pvec[:, 24:32] = pp(inputs["L0_attn_ln2_b"], 8)
    pvec[:, 32:40] = pp(inputs["L0_attn_bo"], 8)
    pvec[:, 40:48] = pp(inputs["L0_attn_b_ff2"], 8)
    pvec[:, 48:56] = pp(inputs["L0_attn_b_d"], 8)
    pvec[:, 56:88] = pp(inputs["L0_attn_b_ff1"], 32)

    cbf = np.zeros((128, 546), np.float32)
    cbf[:, 0] = 1.0
    for c in range(8):
        cbf[:, 2 + 4 * c + c // 2] = 1.0
    for h in range(4):
        cbf[h, 34 + 128 * h: 34 + 128 * (h + 1)] = 1.0
    cbf = cbf.astype(BF)

    shared = dict(
        wq=np.ascontiguousarray(np.asarray(inputs["L0_attn_wq"], np.float32)),
        wk=np.ascontiguousarray(np.asarray(inputs["L0_attn_wk"], np.float32)),
        wv=np.ascontiguousarray(np.asarray(inputs["L0_attn_wv"], np.float32)),
        wo=np.ascontiguousarray(np.asarray(inputs["L0_attn_wo"], np.float32)),
        wff1=np.ascontiguousarray(np.asarray(inputs["L0_attn_w_ff1"], np.float32)),
        wff2=np.ascontiguousarray(np.asarray(inputs["L0_attn_w_ff2"], np.float32)),
        wdd=np.ascontiguousarray(np.asarray(inputs["L0_attn_w_d"], np.float32)),
        wgT=np.ascontiguousarray(np.asarray(inputs["L0_ffn_w_gate"]).astype(BF).T),
        wuT=np.ascontiguousarray(np.asarray(inputs["L0_ffn_w_up"]).astype(BF).T),
        wdT=np.ascontiguousarray(np.asarray(inputs["L0_ffn_w_down"]).astype(BF).T),
        pvec=pvec, cbf=cbf,
    )
    in_maps = []
    for core in range(8):
        bb, half = core // 2, core % 2
        own = x[bb, half * NOWN:(half + 1) * NOWN]
        oth = x[bb, (1 - half) * NOWN:(2 - half) * NOWN]
        m = dict(shared)
        m["xkv"] = np.ascontiguousarray(np.concatenate([own, oth], 0).T)
        in_maps.append(m)
    return in_maps


def kernel(**inputs):
    in_maps = host_inputs(inputs)
    b = build_nc(debug=False)
    res = run_bass_kernel_spmd(b.nc, in_maps, core_ids=list(range(8)))
    out = np.empty((4, N, D), np.float32)
    for core in range(8):
        bb, half = core // 2, core % 2
        y = np.asarray(res.results[core]["y"])
        out[bb, half * NOWN:(half + 1) * NOWN] = y.T
    return out
